# version 01 (final)
# baseline (speedup 1.0000x reference)
import math
import numpy as np

import concourse.bass as bass
import concourse.mybir as mybir
from concourse.bass_core_rust import IndirectOffsetOnAxis
from concourse.bass_utils import run_bass_kernel_spmd

F32 = mybir.dt.float32
BF16 = mybir.dt.bfloat16
I32 = mybir.dt.int32
AF = mybir.ActivationFunctionType
ALU = mybir.AluOpType
AX = mybir.AxisListType

ENGINES = ("tensor", "vector", "scalar", "gpsimd", "sync")

NCORES = 8
TOK = 2048
DM = 1024
DTILES = DM // 128


class Sem:

    def __init__(self, k, name):
        self.h = k.stack.enter_context(k.nc.semaphore(name))
        self.cnt = 0

    def inc(self, instr, n=1):
        self.cnt += n
        instr.then_inc(self.h, n)
        return self.cnt

    def expect(self, n):
        self.cnt += n
        return self.cnt


class Eng:

    def __init__(self, k, eng, name):
        self.k = k
        self.e = eng
        self.name = name
        self.pending = []

    def dma(self, out, in_, sem=None, defer=False):
        s = sem or self.k.dmasem[self.name]
        ins = self.e.dma_start(out=out, in_=in_)
        tgt = s.inc(ins, 16)
        tok = (s, tgt)
        if not defer:
            self.pending.append(tok)
        return tok

    def idma(self, out, out_offset=None, in_=None, in_offset=None, sem=None,
             defer=False, **kw):
        s = sem or self.k.dmasem[self.name]
        ins = self.e.indirect_dma_start(out, out_offset, in_, in_offset, **kw)
        tgt = s.inc(ins, 16)
        tok = (s, tgt)
        if not defer:
            self.pending.append(tok)
        return tok

    def wait(self, tok):
        s, tgt = tok
        assert tgt == s.cnt, (
            f"wait target {tgt} != cumulative {s.cnt}; use a dedicated sem")
        self.e.wait_ge(s.h, tgt)

    def wait_sem(self, sem, tgt=None):
        self.e.wait_ge(sem.h, sem.cnt if tgt is None else tgt)

    def mark(self, instr, n=1):
        self.k.chainsem[self.name].inc(instr, n)
        return instr

    def join(self, eng_name=None):
        s = self.k.chainsem[eng_name or self.name]
        self.e.wait_ge(s.h, s.cnt)


class Kernel:

    def __init__(self, nc, stack):
        self.nc = nc
        self.stack = stack
        self.phases = []
        self.bar = Sem(self, "bar")
        self.dmasem = {n: Sem(self, f"dma_{n}") for n in ("sync", "scalar", "gpsimd", "vector", "tensor")}
        self.chainsem = {n: Sem(self, f"ch_{n}") for n in ENGINES}
        self.sems = {}

    def sem(self, name):
        if name not in self.sems:
            self.sems[name] = Sem(self, name)
        return self.sems[name]

    def phase(self, name=""):
        self.phases.append((name, {e: [] for e in ENGINES}))
        return self

    def on(self, eng, fn):
        self.phases[-1][1][eng].append(fn)
        return self

    def finish(self, block):
        nphase = len(self.phases)
        neng = len(ENGINES)

        def emit_for(eng_name):
            def run(eng_obj):
                w = Eng(self, eng_obj, eng_name)
                for p, (pname, ph) in enumerate(self.phases):
                    if p > 0:
                        eng_obj.wait_ge(self.bar.h, neng * p)
                    for fn in ph[eng_name]:
                        fn(w)
                    finals = {}
                    for s, tgt in w.pending:
                        finals[id(s)] = (s, max(tgt, finals.get(id(s), (s, 0))[1]))
                    for s, tgt in finals.values():
                        eng_obj.wait_ge(s.h, s.cnt)
                    w.pending = []
                    self.bar.expect(1)
                    eng_obj.drain().then_inc(self.bar.h, 1)
                eng_obj.wait_ge(self.bar.h, neng * nphase)
            return run

        block.tensor(emit_for("tensor"))
        block.vector(emit_for("vector"))
        block.scalar(emit_for("scalar"))
        block.gpsimd(emit_for("gpsimd"))
        block.sync(emit_for("sync"))


def ceil_div(a, b):
    return (a + b - 1) // b


def emit_mm(k, name, n_m, n_n, n_k, lhsT, rhs, psums, epilogue,
            evac_eng="vector", extra_mm=None, order="mn"):
    nb = len(psums)
    mm_sem = k.sem(f"mm_{evac_eng}")
    ev_sem = k.sem(f"ev_{evac_eng}")
    ev_base = ev_sem.cnt
    tiles = [(mi, ni) for mi in range(n_m) for ni in range(n_n)] if order == "mn" \
        else [(mi, ni) for ni in range(n_n) for mi in range(n_m)]
    mm_targets = {}
    c = mm_sem.cnt
    for t_i, (mi, ni) in enumerate(tiles):
        c += 1
        mm_targets[(mi, ni)] = c
    mm_sem.expect(len(tiles))
    ev_sem.expect(len(tiles))

    def t_thunk(t):
        cnt = 0
        for t_i, (mi, ni) in enumerate(tiles):
            if t_i >= nb:
                t.e.wait_ge(ev_sem.h, ev_base + t_i - nb + 1)
            ps = psums[t_i % nb]
            for ki in range(n_k):
                ins = t.e.matmul(ps[:, :], lhsT(ki, mi), rhs(ki, ni),
                                 start=(ki == 0), stop=(ki == n_k - 1))
            if extra_mm is not None:
                ins = extra_mm(t, mi, ni, ps) or ins
            ins.then_inc(mm_sem.h, 1)
            cnt += 1

    def e_thunk(e):
        base = mm_targets
        for t_i, (mi, ni) in enumerate(tiles):
            e.e.wait_ge(mm_sem.h, base[(mi, ni)])
            last = epilogue(mi, ni, psums[t_i % nb], e)
            assert last is not None, f"epilogue for {name} must return its last instruction"
            last.then_inc(ev_sem.h, 1)

    k.on("tensor", t_thunk)
    k.on(evac_eng, e_thunk)




def emit_ln(k, name, in_tiles, out_tiles, T, aux, nfeat=None, gb=None, eps=1e-5):
    nk = len(in_tiles)
    D = nfeat or (128 * nk)
    n_n = T // 512
    ones = aux["ones"]
    sq2 = aux["sq2"]
    ps_sum, ps_sq = aux["ps_sum"], aux["ps_sq"]
    s_sem = k.sem("ln_sq")
    m_sem = k.sem("ln_mm")
    v_sem = k.sem("ln_v")
    q_sem = k.sem("ln_sqrt")

    sq_base = s_sem.cnt
    s_sem.expect(nk)
    m_base = m_sem.cnt
    m_sem.expect(nk)
    v_done = v_sem.cnt + 1
    v_sem.expect(1)
    sqrt_done = q_sem.cnt + 1
    q_sem.expect(1)

    def s_thunk(s):
        for ki in range(nk):
            if ki >= 2:
                s.e.wait_ge(m_sem.h, m_base + ki - 1)
            s.e.activation(sq2[ki % 2][:, :], in_tiles[ki][:, :], AF.Square)\
                .then_inc(s_sem.h, 1)

    def t_thunk(t):
        for ki in range(nk):
            for ni in range(n_n):
                t.e.matmul(ps_sum[:, ni * 512:(ni + 1) * 512], ones[:, 0:128],
                           in_tiles[ki][:, ni * 512:(ni + 1) * 512],
                           start=(ki == 0), stop=(ki == nk - 1))
            t.e.wait_ge(s_sem.h, sq_base + ki + 1)
            ins = None
            for ni in range(n_n):
                ins = t.e.matmul(ps_sq[:, ni * 512:(ni + 1) * 512], ones[:, 0:128],
                                 sq2[ki % 2][:, ni * 512:(ni + 1) * 512],
                                 start=(ki == 0), stop=(ki == nk - 1))
            ins.then_inc(m_sem.h, 1)

    mean, var, rstd = aux["mean"], aux["var"], aux["rstd"]

    def v_thunk(v):
        v.e.wait_ge(m_sem.h, m_base + nk)
        v.mark(v.e.tensor_scalar_mul(mean[:, :], ps_sum[:, :], 1.0 / D))
        v.mark(v.e.tensor_scalar_mul(var[:, :], ps_sq[:, :], 1.0 / D))
        v.join()
        v.mark(v.e.tensor_tensor(aux["m2"][:, :], mean[:, :], mean[:, :], op=ALU.mult))
        v.join()
        v.mark(v.e.tensor_sub(var[:, :], var[:, :], aux["m2"][:, :]))
        v.join()
        v.mark(v.e.tensor_scalar_add(var[:, :], var[:, :], eps))
        v.join()
        v.e.reciprocal(aux["v_r"][:, :], var[:, :]).then_inc(v_sem.h, 1)

    def s2_thunk(s):
        s.e.wait_ge(v_sem.h, v_done)
        s.e.activation(rstd[:, :], aux["v_r"][:, :], AF.Sqrt).then_inc(q_sem.h, 1)

    def v2_thunk(v):
        v.e.wait_ge(q_sem.h, sqrt_done)
        for ki in range(nk):
            v.mark(v.e.tensor_tensor(out_tiles[ki][:, :], in_tiles[ki][:, :],
                                     mean[:, :], op=ALU.subtract))
        v.join()
        for ki in range(nk):
            v.mark(v.e.tensor_tensor(out_tiles[ki][:, :], out_tiles[ki][:, :],
                                     rstd[:, :], op=ALU.mult))
        if gb is not None:
            v.join()
            g_t, b_t = gb
            for ki in range(nk):
                v.mark(v.e.tensor_scalar(out_tiles[ki][:, :], out_tiles[ki][:, :],
                                         g_t(ki), b_t(ki), op0=ALU.mult, op1=ALU.add))

    k.on("scalar", s_thunk)
    k.on("tensor", t_thunk)
    k.on("vector", v_thunk)
    k.on("scalar", s2_thunk)
    k.on("vector", v2_thunk)


def ln_aux_alloc(sb, ps, T, tag, in_dtype=F32, ps_sum=None, ps_sq=None):
    aux = {
        "ones": sb(f"lnones_{tag}", [128, 128], in_dtype),
        "sq2": [sb(f"lnsq{i}_{tag}", [128, T], in_dtype) for i in range(2)],
        "mean": sb(f"lnmean_{tag}", [128, T]),
        "var": sb(f"lnvar_{tag}", [128, T]),
        "rstd": sb(f"lnrstd_{tag}", [128, T]),
        "m2": sb(f"lnm2_{tag}", [128, T]),
        "v_r": sb(f"lnvr_{tag}", [128, T]),
    }
    aux["ps_sum"] = ps_sum if ps_sum is not None else ps(f"lnps_a_{tag}", [128, T])
    aux["ps_sq"] = ps_sq if ps_sq is not None else ps(f"lnps_b_{tag}", [128, T])
    return aux


TERN_SPECS = [
    ("L0w1", 1024, 4096, 1e-5, True),
    ("L0w2", 4096, 1024, 1e-5, True),
    ("L2w1", 1024, 4096, 1e-8, False),
    ("L2w2", 4096, 1024, 1e-8, False),
]


CAST_SPECS = [
    ("M1w1", 8192, 4096),
    ("M1w2", 32768, 1024),
]


def build_prep():
    import contextlib
    nc = bass.Bass("TRN2", target_bir_lowering=False)
    full, own, out = {}, {}, {}
    for name, rows, cols, clip, resc in TERN_SPECS:
        full[name] = nc.dram_tensor(name, [rows, cols], F32, kind="ExternalInput").ap()
        own[name] = nc.dram_tensor("own_" + name, [rows // 8, cols], F32,
                                   kind="ExternalInput").ap()
        out[name] = nc.dram_tensor(name + "_q", [rows // 8, cols], BF16,
                                   kind="ExternalOutput").ap()
    for name, rows, cols in CAST_SPECS:
        own[name] = nc.dram_tensor("own_" + name, [rows // 8, cols], F32,
                                   kind="ExternalInput").ap()
        out[name] = nc.dram_tensor(name + "_c", [rows // 8, cols], BF16,
                                   kind="ExternalOutput").ap()

    with contextlib.ExitStack() as stack:
        sb = lambda n, s, dt=F32: stack.enter_context(nc.sbuf_tensor(n, s, dt))
        ps = lambda n, s, dt=F32: stack.enter_context(nc.psum_tensor(n, s, dt))
        wt = [sb(f"wt{i}", [128, 4096]) for i in range(2)]
        awt = [sb(f"awt{i}", [128, 4096]) for i in range(2)]
        part = sb("part", [128, 64])
        ones1 = sb("ones1", [128, 1])
        onesr = sb("onesr", [1, 128])
        scal = sb("scal", [1, 4])
        scal_b = sb("scal_b", [128, 2])
        myw = sb("myw", [128, 4096])
        q1 = sb("q1", [128, 4096])
        q2 = sb("q2", [128, 4096])
        outw = sb("outw", [128, 4096], BF16)
        ps_c = ps("ps_c", [1, 64])
        ps_b = ps("ps_b", [128, 2])

        with nc.Block() as block:
            k = Kernel(nc, stack)
            k.phase("init")
            k.on("vector", lambda v: (v.e.memset(ones1[:, :], 1.0), v.e.memset(onesr[:, :], 1.0)))

            for wi, (name, rows, cols, clip, resc) in enumerate(TERN_SPECS):
                w_ap = full[name]
                nt = rows // 128
                ld = [k.sem("tld0"), k.sem("tld1")]
                ab = k.sem("tab")
                rd = k.sem("trd")
                k.phase(f"abs_{name}")
                ld_base = [ld[0].cnt, ld[1].cnt]
                for ti in range(nt):
                    ld[ti % 2].expect(16)
                ab_base = ab.cnt
                ab.expect(nt)
                rd_base = rd.cnt
                rd.expect(nt)

                def mk_load(w_ap=w_ap, nt=nt, cols=cols, ld=ld, ab=ab, ab_base=ab_base):
                    def f(sy):
                        for ti in range(nt):
                            if ti >= 2:
                                sy.e.wait_ge(ab.h, ab_base + ti - 1)
                            sy.e.dma_start(out=wt[ti % 2][:, 0:cols],
                                           in_=w_ap[ti * 128:(ti + 1) * 128, :])\
                                .then_inc(ld[ti % 2].h, 16)
                        sy.e.wait_ge(ab.h, ab_base + nt)
                    return f

                def mk_abs(nt=nt, cols=cols, ld=ld, ab=ab, rd=rd,
                           ld_base=ld_base, rd_base=rd_base):
                    def fs(s):
                        for ti in range(nt):
                            s.e.wait_ge(ld[ti % 2].h, ld_base[ti % 2] + 16 * (ti // 2 + 1))
                            if ti >= 2:
                                s.e.wait_ge(rd.h, rd_base + ti - 1)
                            s.e.activation(awt[ti % 2][:, 0:cols], wt[ti % 2][:, 0:cols],
                                           AF.Abs).then_inc(ab.h, 1)
                    return fs

                def mk_red(nt=nt, cols=cols, ab=ab, rd=rd, ab_base=ab_base):
                    def fv(v):
                        for ti in range(nt):
                            v.e.wait_ge(ab.h, ab_base + ti + 1)
                            v.e.reduce_sum(part[:, ti:ti + 1], awt[ti % 2][:, 0:cols],
                                           axis=AX.X).then_inc(rd.h, 1)
                    return fv

                k.on("sync", mk_load())
                k.on("scalar", mk_abs())
                k.on("vector", mk_red())

                k.phase(f"scal_{name}")
                ttot = k.sem("ttot")
                t_done = ttot.cnt + 1
                ttot.expect(1)
                tsc = k.sem("tsc")
                sc_done = tsc.cnt + 1
                tsc.expect(1)
                tbc = k.sem("tbc")
                bc_done = tbc.cnt + 1
                tbc.expect(1)

                def mk_tot(nt=nt, ttot=ttot):
                    def ft(t):
                        t.e.matmul(ps_c[:, 0:nt], ones1[:, 0:1], part[:, 0:nt],
                                   start=True, stop=True).then_inc(ttot.h, 1)
                    return ft

                def mk_sc(nt=nt, rows=rows, cols=cols, clip=clip, ttot=ttot,
                          t_done=t_done, tsc=tsc):
                    def fv(v):
                        v.e.wait_ge(ttot.h, t_done)
                        v.mark(v.e.reduce_sum(scal[0:1, 0:1], ps_c[0:1, 0:nt], axis=AX.X))
                        v.join()
                        v.mark(v.e.tensor_scalar(scal[0:1, 1:2], scal[0:1, 0:1],
                                                 1.0 / (rows * cols), clip,
                                                 op0=ALU.mult, op1=ALU.max))
                        v.join()
                        v.e.reciprocal(scal[0:1, 2:3], scal[0:1, 1:2]).then_inc(tsc.h, 1)
                    return fv

                def mk_bc(tsc=tsc, sc_done=sc_done, tbc=tbc):
                    def ft(t):
                        t.e.wait_ge(tsc.h, sc_done)
                        t.e.matmul(ps_b[:, 0:2], onesr[:, :], scal[0:1, 1:3],
                                   start=True, stop=True).then_inc(tbc.h, 1)
                    return ft
                k.on("tensor", mk_tot())
                k.on("vector", mk_sc())
                k.on("tensor", mk_bc())
                k.phase(f"bcast_{name}")
                k.on("vector", lambda v: v.e.tensor_copy(scal_b[:, 0:2], ps_b[:, 0:2]))

                for si in range(rows // 8 // 128):
                    k.phase(f"q_{name}_{si}")
                    k.on("gpsimd", (lambda own_ap=own[name], si=si, cols=cols:
                                    (lambda g: g.dma(myw[:, 0:cols],
                                                     own_ap[si * 128:(si + 1) * 128, :])))())
                    k.phase(f"q2_{name}_{si}")

                    def mk_q(cols=cols, resc=resc):
                        def fv(v):
                            v.mark(v.e.tensor_scalar_mul(q1[:, 0:cols], myw[:, 0:cols],
                                                         scal_b[:, 1:2]))
                            v.join()
                            v.mark(v.e.tensor_single_scalar(q2[:, 0:cols], q1[:, 0:cols],
                                                            0.5, op=ALU.is_gt))
                            v.mark(v.e.tensor_single_scalar(q1[:, 0:cols], q1[:, 0:cols],
                                                            -0.5, op=ALU.is_lt))
                            v.join()
                            v.mark(v.e.tensor_sub(q2[:, 0:cols], q2[:, 0:cols],
                                                  q1[:, 0:cols]))
                            v.join()
                            if resc:
                                v.mark(v.e.tensor_scalar_mul(outw[:, 0:cols],
                                                             q2[:, 0:cols],
                                                             scal_b[:, 0:1]))
                            else:
                                v.mark(v.e.tensor_copy(outw[:, 0:cols], q2[:, 0:cols]))
                        return fv
                    k.on("vector", mk_q())
                    k.phase(f"st_{name}_{si}")
                    k.on("sync", (lambda out_ap=out[name], si=si, cols=cols:
                                  (lambda sy: sy.dma(out_ap[si * 128:(si + 1) * 128, :],
                                                     outw[:, 0:cols])))())

            for name, rows, cols in CAST_SPECS:
                for ti in range(rows // 8 // 128):
                    k.phase(f"cld_{name}_{ti}")
                    k.on("gpsimd", (lambda own_ap=own[name], ti=ti, cols=cols:
                                    (lambda g: g.dma(outw[:, 0:cols],
                                                     own_ap[ti * 128:(ti + 1) * 128, :])))())
                    k.phase(f"cst_{name}_{ti}")
                    k.on("sync", (lambda out_ap=out[name], ti=ti, cols=cols:
                                  (lambda sy: sy.dma(out_ap[ti * 128:(ti + 1) * 128, :],
                                                     outw[:, 0:cols])))())
            k.finish(block)
    return nc


def run_prep(inputs, runner):
    import ml_dtypes
    nc = build_prep()
    w = {
        "L0w1": np.ascontiguousarray(np.asarray(inputs["L0_ffn_w1"]).T),
        "L0w2": np.ascontiguousarray(np.asarray(inputs["L0_ffn_w2"]).T),
        "L2w1": np.ascontiguousarray(np.asarray(inputs["L2_ffn_W1"]).T),
        "L2w2": np.ascontiguousarray(np.asarray(inputs["L2_ffn_W2"]).T),
    }
    wc = {
        "M1w1": np.asarray(inputs["L1_ffn_W1"]).reshape(8192, 4096),
        "M1w2": np.asarray(inputs["L1_ffn_W2"]).reshape(32768, 1024),
    }
    in_maps = []
    for c in range(NCORES):
        m = {}
        for name, rows, cols, clip, resc in TERN_SPECS:
            m[name] = w[name]
            sl = rows // 8
            m["own_" + name] = np.ascontiguousarray(w[name][c * sl:(c + 1) * sl, :])
        for name, rows, cols in CAST_SPECS:
            sl = rows // 8
            m["own_" + name] = np.ascontiguousarray(wc[name][c * sl:(c + 1) * sl, :])
        in_maps.append(m)
    res = runner(nc, in_maps)
    outq = {}
    for name, rows, cols, clip, resc in TERN_SPECS:
        outq[name + "_q"] = np.concatenate(
            [res[c][name + "_q"] for c in range(NCORES)], axis=0)
    for name, rows, cols in CAST_SPECS:
        outq[name + "_c"] = np.concatenate(
            [res[c][name + "_c"] for c in range(NCORES)], axis=0)
    return outq


def emit_ffn_step(k, kind, sb, ps, io):
    TC = 1024
    NCH = TOK // TC
    has_ln1 = kind == "L0"

    Xc = [sb(f"Xc{i}", [128, TC]) for i in range(8)]
    XB = [sb(f"XB{i}", [128, TC], BF16) for i in range(8)]
    XA = [sb(f"XA{i}", [128, TC], BF16) for i in range(8)] if has_ln1 else None
    hb = sb("hb", [128, 32 * TC], BF16)
    TMP = sb("TMP", [128, 4 * TC])
    wbuf = [sb(f"wbuf{i}", [128, 4096], BF16) for i in range(2)]
    b1s = sb("b1s", [128, 32])
    b2s = sb("b2s", [128, 8])
    b2h = sb("b2h", [128, 8])
    if has_ln1:
        ln1g = sb("ln1g_s", [128, 8])
        ln1b = sb("ln1b_s", [128, 8])
        ln2g = sb("ln2g_s", [128, 32])
        ln2b = sb("ln2b_s", [128, 32])

    ps_a = ps("ps_a", [128, TC])
    ps_b = ps("ps_b", [128, TC])
    ps_c = [ps(f"ps_c{i}", [128, 512]) for i in range(4)]

    auxf = ln_aux_alloc(sb, ps, TC, "f", F32, ps_sum=ps_a, ps_sq=ps_b)
    auxb = dict(auxf)
    auxb["ones"] = sb("lnones_b", [128, 128], BF16)
    auxb["sq2"] = [sb(f"lnsqb{i}", [128, TC], BF16) for i in range(2)]

    k.phase("ffn_init")

    def init_ld(sy):
        sy.dma(b1s[:, :], io["b1r"][:, :])
        sy.dma(b2s[:, :], io["b2r"][:, :])
        if has_ln1:
            sy.dma(ln1g[:, :], io["ln1g"][:, :])
            sy.dma(ln1b[:, :], io["ln1b"][:, :])
            sy.dma(ln2g[:, :], io["ln2g"][:, :])
            sy.dma(ln2b[:, :], io["ln2b"][:, :])
    k.on("sync", init_ld)
    k.on("vector", lambda v: (v.e.memset(auxf["ones"][:, :], 1.0),
                              v.e.memset(auxb["ones"][:, :], 1.0)))
    k.phase("ffn_init2")
    k.on("vector", lambda v: v.e.tensor_scalar_mul(b2h[:, :], b2s[:, :], 0.5))

    for tc in range(NCH):
        tsl = slice(tc * TC, (tc + 1) * TC)
        k.phase(f"ffn_ld{tc}")

        def ld_x(sy, tsl=tsl):
            for ki in range(8):
                sy.dma(Xc[ki][:, :], io["xT"][ki * 128:(ki + 1) * 128, tsl])
        k.on("sync", ld_x)

        k.phase(f"ffn_lnA{tc}")
        if has_ln1:
            emit_ln(k, f"lnA{tc}", Xc, XA, TC, auxf)
            k.phase(f"ffn_lnB{tc}")
            emit_ln(k, f"lnB{tc}", XA, XB, TC, auxb,
                    gb=(lambda ki: ln1g[:, ki:ki + 1], lambda ki: ln1b[:, ki:ki + 1]))
        else:
            emit_ln(k, f"lnA{tc}", Xc, XB, TC, auxf)

        k.phase(f"ffn_mm1{tc}")
        wl = [k.sem("ffn_wl0"), k.sem("ffn_wl1")]
        pt = k.sem("ffn_mm1t")
        ev = k.sem("ffn_mm1e")
        wl_base = [wl[0].cnt, wl[1].cnt]
        pt_base = pt.cnt
        ev_base = ev.cnt
        for og in range(8):
            wl[og % 2].expect(8 * 16)
        pt.expect(64)
        ev.expect(64)

        def mm1_ld(sy, tc=tc, wl=wl, wl_base=wl_base, pt=pt, pt_base=pt_base):
            for og in range(8):
                if og >= 2:
                    sy.e.wait_ge(pt.h, pt_base + 8 * (og - 1))
                for ki in range(8):
                    sy.e.dma_start(
                        out=wbuf[og % 2][:, ki * 512:(ki + 1) * 512],
                        in_=io["w1q"][ki * 128:(ki + 1) * 128, og * 512:(og + 1) * 512],
                    ).then_inc(wl[og % 2].h, 16)
            sy.e.wait_ge(pt.h, pt_base + 64)

        def mm1_mm(t, wl=wl, wl_base=wl_base, pt=pt, pt_base=pt_base, ev=ev,
                   ev_base=ev_base):
            tg = 0
            for og in range(8):
                t.e.wait_ge(wl[og % 2].h, wl_base[og % 2] + 128 * (og // 2 + 1))
                for mi in range(4):
                    for ni in range(2):
                        if tg >= 4:
                            t.e.wait_ge(ev.h, ev_base + tg - 3)
                        bank = ps_c[tg % 4]
                        ins = None
                        for ki in range(8):
                            ins = t.e.matmul(
                                bank[:, :],
                                wbuf[og % 2][:, ki * 512 + mi * 128: ki * 512 + (mi + 1) * 128],
                                XB[ki][:, ni * 512:(ni + 1) * 512],
                                start=(ki == 0), stop=(ki == 7))
                        ins.then_inc(pt.h, 1)
                        tg += 1

        def mm1_ep(s, pt=pt, pt_base=pt_base, ev=ev):
            tg = 0
            for og in range(8):
                for mi in range(4):
                    mg = og * 4 + mi
                    for ni in range(2):
                        s.e.wait_ge(pt.h, pt_base + tg + 1)
                        bank = ps_c[tg % 4]
                        s.e.activation(
                            hb[:, mg * TC + ni * 512: mg * TC + (ni + 1) * 512],
                            bank[:, :], AF.Gelu, bias=b1s[:, mg:mg + 1], scale=1.0,
                        ).then_inc(ev.h, 1)
                        tg += 1
        k.on("sync", mm1_ld)
        k.on("tensor", mm1_mm)
        k.on("scalar", mm1_ep)

        if has_ln1:
            k.phase(f"ffn_ln2{tc}")
            hb_tiles = [hb[:, mg * TC:(mg + 1) * TC] for mg in range(32)]
            emit_ln(k, f"ln2{tc}", hb_tiles, hb_tiles, TC, auxb, nfeat=4096,
                    gb=(lambda ki: ln2g[:, ki:ki + 1], lambda ki: ln2b[:, ki:ki + 1]))

        k.phase(f"ffn_mm2{tc}")
        w2l = [k.sem("ffn_w2l0"), k.sem("ffn_w2l1")]
        kcp = k.sem("ffn_kcp")
        epd = k.sem("ffn_mm2e")
        res = k.sem("ffn_res")
        std = k.sem("ffn_st")
        w2l_base = [w2l[0].cnt, w2l[1].cnt]
        kcp_base = kcp.cnt
        epd_base = epd.cnt
        res_base = res.cnt
        for mg in range(2):
            for kc in range(4):
                w2l[kc % 2].expect(8 * 16)
        kcp.expect(8)
        epd.expect(16)
        res.expect(2)

        banks8 = [ps_a[:, 0:512], ps_a[:, 512:1024], ps_b[:, 0:512],
                  ps_b[:, 512:1024], ps_c[0][:, :], ps_c[1][:, :],
                  ps_c[2][:, :], ps_c[3][:, :]]

        def mm2_ld(sy, w2l=w2l, kcp=kcp, kcp_base=kcp_base, res=res,
                   res_base=res_base):
            n = 0
            for mg in range(2):
                for kc in range(4):
                    if n >= 2:
                        sy.e.wait_ge(kcp.h, kcp_base + n - 1)
                    for kk in range(8):
                        sy.e.dma_start(
                            out=wbuf[kc % 2][:, kk * 512:(kk + 1) * 512],
                            in_=io["w2q"][kc * 1024 + kk * 128: kc * 1024 + (kk + 1) * 128,
                                          mg * 512:(mg + 1) * 512],
                        ).then_inc(w2l[kc % 2].h, 16)
                    n += 1
            sy.e.wait_ge(kcp.h, kcp_base + 8)

        def mm2_mm(t, w2l=w2l, w2l_base=w2l_base, kcp=kcp,
                   epd=epd, epd_base=epd_base):
            n = 0
            wl_cnt = [w2l_base[0], w2l_base[1]]
            for mg in range(2):
                if mg == 1:
                    t.e.wait_ge(epd.h, epd_base + 8)
                for kc in range(4):
                    wl_cnt[kc % 2] += 128
                    t.e.wait_ge(w2l[kc % 2].h, wl_cnt[kc % 2])
                    ins = None
                    for kk in range(8):
                        for mi in range(4):
                            for ni in range(2):
                                ins = t.e.matmul(
                                    banks8[mi * 2 + ni][:, :],
                                    wbuf[kc % 2][:, kk * 512 + mi * 128: kk * 512 + (mi + 1) * 128],
                                    hb[:, (kc * 8 + kk) * TC + ni * 512:
                                       (kc * 8 + kk) * TC + (ni + 1) * 512],
                                    start=(kc == 0 and kk == 0),
                                    stop=(kc == 3 and kk == 7))
                    ins.then_inc(kcp.h, 1)
                    n += 1

        def mm2_ep(s, kcp=kcp, kcp_base=kcp_base, epd=epd, res=res,
                   res_base=res_base):
            for mg in range(2):
                if mg == 1:
                    s.e.wait_ge(res.h, res_base + 1)
                s.e.wait_ge(kcp.h, kcp_base + 4 * (mg + 1))
                for mi in range(4):
                    for ni in range(2):
                        s.e.activation(
                            TMP[:, mi * TC + ni * 512: mi * TC + (ni + 1) * 512],
                            banks8[mi * 2 + ni][:, :], AF.Identity,
                            bias=b2h[:, mg * 4 + mi:mg * 4 + mi + 1], scale=0.5,
                        ).then_inc(epd.h, 1)

        def mm2_res(v, epd=epd, epd_base=epd_base, res=res):
            for mg in range(2):
                v.e.wait_ge(epd.h, epd_base + 8 * (mg + 1))
                ins = None
                for mi in range(4):
                    ins = v.e.tensor_add(Xc[mg * 4 + mi][:, :], Xc[mg * 4 + mi][:, :],
                                         TMP[:, mi * TC:(mi + 1) * TC])
                ins.then_inc(res.h, 1)

        def mm2_st(g, res=res, res_base=res_base, tsl=tsl):
            for mg in range(2):
                g.e.wait_ge(res.h, res_base + mg + 1)
                for mi in range(4):
                    ki = mg * 4 + mi
                    g.dma(io["xoT"][ki * 128:(ki + 1) * 128, tsl], Xc[ki][:, :])
        k.on("sync", mm2_ld)
        k.on("tensor", mm2_mm)
        k.on("scalar", mm2_ep)
        k.on("vector", mm2_res)
        k.on("gpsimd", mm2_st)


def build_ffn_step(kind):
    import contextlib
    nc = bass.Bass("TRN2", target_bir_lowering=False)
    io = {}
    io["xT"] = nc.dram_tensor("xT", [DM, TOK], F32, kind="ExternalInput").ap()
    io["xoT"] = nc.dram_tensor("xoT", [DM, TOK], F32, kind="ExternalOutput").ap()
    io["w1q"] = nc.dram_tensor("w1q", [1024, 4096], BF16, kind="ExternalInput").ap()
    io["w2q"] = nc.dram_tensor("w2q", [4096, 1024], BF16, kind="ExternalInput").ap()
    io["b1r"] = nc.dram_tensor("b1r", [128, 32], F32, kind="ExternalInput").ap()
    io["b2r"] = nc.dram_tensor("b2r", [128, 8], F32, kind="ExternalInput").ap()
    if kind == "L0":
        for nm in ("ln1g", "ln1b"):
            io[nm] = nc.dram_tensor(nm, [128, 8], F32, kind="ExternalInput").ap()
        for nm in ("ln2g", "ln2b"):
            io[nm] = nc.dram_tensor(nm, [128, 32], F32, kind="ExternalInput").ap()
    with contextlib.ExitStack() as stack:
        sb = lambda n, s, dt=F32: stack.enter_context(nc.sbuf_tensor(n, s, dt))
        ps = lambda n, s, dt=F32: stack.enter_context(nc.psum_tensor(n, s, dt))
        with nc.Block() as block:
            k = Kernel(nc, stack)
            emit_ffn_step(k, kind, sb, ps, io)
            k.finish(block)
    return nc


def col_major(v, p=128):
    v = np.asarray(v, np.float32)
    return np.ascontiguousarray(v.reshape(-1, p).T)


def emit_l3f_step(k, sb, ps, io):
    T = TOK
    Xc = [sb(f"3Xc{i}", [128, T]) for i in range(8)]
    XB = [sb(f"3XB{i}", [128, T], BF16) for i in range(8)]
    TMP = sb("3TMP", [128, 4 * 512])
    g1 = {w: sb(f"3g1{w}", [32, 4 * 512]) for w in "RI"}
    g2 = {w: sb(f"3g2{w}", [16, 4 * 512]) for w in "RI"}
    Ed = sb("3Ed", [32, 512])
    Eg = sb("3Eg", [16, 512])
    Wt = {w: sb(f"3Wt{w}", [128, 4 * 512]) for w in "RI"}
    Wb = {w: sb(f"3Wb{w}", [128, 4 * 512], BF16) for w in "RI"}
    Wni = sb("3Wni", [128, 4 * 512], BF16)
    exA = sb("3exA", [128, 512])
    exB = sb("3exB", [128, 512])
    exM = sb("3exM", [128, 512])
    brs = sb("3brs", [128, 8])
    ps_a = ps("3ps_a", [128, T])
    ps_b = ps("3ps_b", [128, T])
    ps_c = [ps_b[:, i * 512:(i + 1) * 512] for i in range(4)]
    lnA = sb("3lnA", [128, T])
    lnB = sb("3lnB", [128, T])
    auxf = {"ones": sb("3lno", [128, 128]), "sq2": [Wt["R"], Wt["I"]],
            "mean": TMP[:, 0:T], "var": lnA, "m2": lnB, "v_r": lnB,
            "rstd": lnA, "ps_sum": ps_a, "ps_sq": ps_b}

    k.phase("l3f_init")

    def init_ld(sy):
        for w in "RI":
            sy.dma(g1[w][:, :], io["g1" + w][:, :])
            sy.dma(g2[w][:, :], io["g2" + w][:, :])
        sy.dma(Ed[:, :], io["Ed"][:, :])
        sy.dma(Eg[:, :], io["Eg"][:, :])
        sy.dma(brs[:, :], io["brs"][:, :])
        for ki in range(8):
            sy.dma(Xc[ki][:, :], io["xT"][ki * 128:(ki + 1) * 128, :])
    k.on("sync", init_ld)
    k.on("vector", lambda v: v.e.memset(auxf["ones"][:, :], 1.0))

    kb = k.sem("l3f_kron")
    kb_base = kb.cnt
    kb.expect(64)
    wv = k.sem("l3f_wv")
    wv_base = wv.cnt
    wv.expect(32)

    k.phase("l3f_kron")

    def kron_t(t):
        i = 0
        for w in "RI":
            for c in range(4):
                rhsA = g1[w][:, c * 512:(c + 1) * 512]
                rhsB = g2[w][:, c * 512:(c + 1) * 512]
                for rt in range(4):
                    if i >= 1:
                        t.e.wait_ge(wv.h, wv_base + i)
                    t.e.matmul(ps_c[0], Ed[:, rt * 128:(rt + 1) * 128],
                               rhsA, start=True, stop=True).then_inc(kb.h, 1)
                    t.e.matmul(ps_c[1], Eg[:, rt * 128:(rt + 1) * 128],
                               rhsB, start=True, stop=True).then_inc(kb.h, 1)
                    i += 1

    def kron_v(v):
        i = 0
        for w in "RI":
            for c in range(4):
                for rt in range(4):
                    v.e.wait_ge(kb.h, kb_base + 2 * (i + 1))
                    v.mark(v.e.tensor_copy(exA[:, :], ps_c[0]))
                    v.mark(v.e.tensor_copy(exB[:, :], ps_c[1]))
                    v.join()
                    dst = Wt[w][:, rt * 512:(rt + 1) * 512]
                    if c == 0:
                        ins = v.e.tensor_tensor(dst, exA[:, :], exB[:, :], op=ALU.mult)
                    else:
                        v.mark(v.e.tensor_tensor(exM[:, :], exA[:, :], exB[:, :],
                                                 op=ALU.mult))
                        v.join()
                        ins = v.e.tensor_add(dst, dst, exM[:, :])
                    ins.then_inc(wv.h, 1)
                    i += 1
    k.on("tensor", kron_t)
    k.on("vector", kron_v)

    k.phase("l3f_wcast")

    def wcast(v):
        for w in "RI":
            v.e.tensor_copy(Wb[w][:, :], Wt[w][:, :])
        v.e.tensor_scalar_mul(Wni[:, :], Wt["I"][:, :], -1.0)
    k.on("vector", wcast)

    k.phase("l3f_ln")
    emit_ln(k, "l3f_ln", Xc, XB, T, auxf)

    k.phase("l3f_mm")
    mt = k.sem("l3f_mt")
    me = k.sem("l3f_me")
    rs = k.sem("l3f_rs")
    mt_base, me_base, rs_base = mt.cnt, me.cnt, rs.cnt
    mt.expect(32)
    me.expect(32)
    rs.expect(32)

    def mm_t(t):
        tg = 0
        for mo in range(8):
            real = mo < 4
            mm = mo % 4
            w1 = Wb["R"] if real else Wb["I"]
            w2 = Wni if real else Wb["R"]
            for ni in range(4):
                if tg >= 4:
                    t.e.wait_ge(me.h, me_base + tg - 3)
                bank = ps_c[tg % 4]
                ins = None
                for kt in range(4):
                    t.e.matmul(bank,
                               w1[:, kt * 512 + mm * 128: kt * 512 + (mm + 1) * 128],
                               XB[kt][:, ni * 512:(ni + 1) * 512],
                               start=(kt == 0), stop=False)
                    ins = t.e.matmul(bank,
                                     w2[:, kt * 512 + mm * 128: kt * 512 + (mm + 1) * 128],
                                     XB[4 + kt][:, ni * 512:(ni + 1) * 512],
                                     start=False, stop=(kt == 3))
                ins.then_inc(mt.h, 1)
                tg += 1

    def mm_e(s):
        tg = 0
        for mo in range(8):
            for ni in range(4):
                s.e.wait_ge(mt.h, mt_base + tg + 1)
                if tg >= 4:
                    s.e.wait_ge(rs.h, rs_base + tg - 3)
                s.e.activation(TMP[:, (tg % 4) * 512:(tg % 4 + 1) * 512],
                               ps_c[tg % 4], AF.Identity,
                               bias=brs[:, mo:mo + 1], scale=0.5).then_inc(me.h, 1)
                tg += 1

    def mm_r(v):
        tg = 0
        for mo in range(8):
            for ni in range(4):
                v.e.wait_ge(me.h, me_base + tg + 1)
                v.e.tensor_add(Xc[mo][:, ni * 512:(ni + 1) * 512],
                               Xc[mo][:, ni * 512:(ni + 1) * 512],
                               TMP[:, (tg % 4) * 512:(tg % 4 + 1) * 512])\
                    .then_inc(rs.h, 1)
                tg += 1
    k.on("tensor", mm_t)
    k.on("scalar", mm_e)
    k.on("vector", mm_r)

    k.phase("l3f_st")

    def st(g):
        for ki in range(8):
            g.dma(io["xoT"][ki * 128:(ki + 1) * 128, :], Xc[ki][:, :])
    k.on("gpsimd", st)


def build_l3f_step():
    import contextlib
    nc = bass.Bass("TRN2", target_bir_lowering=False)
    io = {}
    io["xT"] = nc.dram_tensor("xT", [DM, TOK], F32, kind="ExternalInput").ap()
    io["xoT"] = nc.dram_tensor("xoT", [DM, TOK], F32, kind="ExternalOutput").ap()
    for nm, shp in (("g1R", [32, 2048]), ("g1I", [32, 2048]), ("g2R", [16, 2048]),
                    ("g2I", [16, 2048]), ("Ed", [32, 512]), ("Eg", [16, 512]),
                    ("brs", [128, 8])):
        io[nm] = nc.dram_tensor(nm, shp, F32, kind="ExternalInput").ap()
    with contextlib.ExitStack() as stack:
        sb = lambda n, s, dt=F32: stack.enter_context(nc.sbuf_tensor(n, s, dt))
        ps = lambda n, s, dt=F32: stack.enter_context(nc.psum_tensor(n, s, dt))
        with nc.Block() as block:
            k = Kernel(nc, stack)
            emit_l3f_step(k, sb, ps, io)
            k.finish(block)
    return nc


def l3f_host_inputs(inp):
    g1R = np.asarray(inp["L3_ffn_G1_R"], np.float32)[0]
    g1I = np.asarray(inp["L3_ffn_G1_I"], np.float32)[0]
    g2R = np.asarray(inp["L3_ffn_G2_R"], np.float32)[:, :, 0]
    g2I = np.asarray(inp["L3_ffn_G2_I"], np.float32)[:, :, 0]
    def expA(g):
        return np.concatenate(
            [np.repeat(g[:, c, :].T, 16, axis=1) for c in range(4)], axis=1)

    def expB(g):
        return np.concatenate(
            [np.tile(g[c].T, (1, 32)) for c in range(4)], axis=1)
    g1Rt, g1It = expA(g1R), expA(g1I)
    g2Rt, g2It = expB(g2R), expB(g2I)
    r = np.arange(512)
    Ed = (r[None, :] // 16 == np.arange(32)[:, None]).astype(np.float32)
    Eg = (r[None, :] % 16 == np.arange(16)[:, None]).astype(np.float32)
    br = np.concatenate([np.asarray(inp["L3_ffn_bias_real"], np.float32),
                         np.asarray(inp["L3_ffn_bias_imag"], np.float32)])
    return {"g1R": np.ascontiguousarray(g1Rt), "g1I": np.ascontiguousarray(g1It),
            "g2R": np.ascontiguousarray(g2Rt), "g2I": np.ascontiguousarray(g2It),
            "Ed": Ed, "Eg": Eg, "brs": col_major(0.5 * br)}


def emit_l2a_A(k, sb, ps, io):
    TC = 512
    NCH = TOK // TC
    Xc = [sb(f"2Xc{i}", [128, TC]) for i in range(8)]
    XH = [sb(f"2XH{i}", [128, TC]) for i in range(8)]
    XB = [sb(f"2XB{i}", [128, TC], BF16) for i in range(8)]
    XS = [sb(f"2XS{i}", [128, TC]) for i in range(8)]
    XSB = [sb(f"2XSB{i}", [128, TC], BF16) for i in range(8)]
    ZT = [sb(f"2ZT{i}", [128, TC]) for i in range(8)]
    DT = [sb(f"2DT{i}", [128, TC]) for i in range(8)]
    SPP = [sb(f"2SPP{i}", [128, TC]) for i in range(8)]
    DTX = SPP
    CS = Xc
    YL = [sb(f"2YL{i}", [128, TC]) for i in range(8)]
    winb = sb("2winb", [128, 8 * 2048], BF16)
    wdtb = sb("2wdtb", [128, 8 * 1024], BF16)
    wBb = sb("2wBb", [128, 8 * 16], BF16)
    wCb = sb("2wCb", [128, 8 * 16], BF16)
    bx = sb("2bx", [128, 8])
    bz = sb("2bz", [128, 8])
    bdt = sb("2bdt", [128, 8])
    bB = sb("2bB", [16, 1])
    bC = sb("2bC", [16, 1])
    SEL = sb("2SEL", [16, 2048])
    Afm = sb("2Afm", [128, 128])
    Alog = sb("2Alog", [128, 128])
    Ptot = sb("2Ptot", [128, 128])
    hstate = sb("2hstate", [128, 128])
    carry = sb("2carry", [128, 8])
    BmT = sb("2BmT", [16, TC])
    CmT = sb("2CmT", [16, TC])
    onest = sb("2onest", [128, TC])
    Bb = sb("2Bb", [128, TC])
    Cb = sb("2Cb", [128, TC])
    abuf = sb("2abuf", [128, TC])
    ubuf = sb("2ubuf", [128, TC])
    hbuf = sb("2hbuf", [128, TC])
    tbuf = sb("2tbuf", [128, TC])
    sp2 = Bb
    sp3 = Cb
    Dfm = sb("2Dfm", [128, 8])
    ps_a = ps("2ps_a", [128, TC])
    ps_b = ps("2ps_b", [128, TC])
    ps_c = [ps(f"2ps_c{i}", [128, TC]) for i in range(4)]
    ps_s = ps("2ps_s", [16, TC])
    ps_r = ps("2ps_r", [128, TC])
    auxf = {"ones": sb("2lno", [128, 128]), "sq2": [abuf, ubuf],
            "mean": ZT[0], "var": ZT[1], "rstd": ZT[2], "m2": tbuf, "v_r": hbuf,
            "ps_sum": ps_a, "ps_sq": ps_b}

    k.phase("l2a_init")

    def init_ld(g):
        for ki in range(8):
            g.dma(winb[:, ki * 2048:(ki + 1) * 2048],
                  io["w_inT"][ki * 128:(ki + 1) * 128, :])
            g.dma(wdtb[:, ki * 1024:(ki + 1) * 1024],
                  io["w_dtT"][ki * 128:(ki + 1) * 128, :])
            g.dma(wBb[:, ki * 16:(ki + 1) * 16],
                  io["w_BT"][ki * 128:(ki + 1) * 128, :])
            g.dma(wCb[:, ki * 16:(ki + 1) * 16],
                  io["w_CT"][ki * 128:(ki + 1) * 128, :])
    k.on("gpsimd", init_ld)

    def init_ld2(sy):
        for nm, t in (("bx", bx), ("bz", bz), ("bdt", bdt), ("bB", bB), ("bC", bC),
                      ("A_fm", Alog), ("D_fm", Dfm), ("SEL", SEL)):
            sy.dma(t[:, :], io[nm][:, :])
    k.on("sync", init_ld2)
    k.on("vector", lambda v: (v.e.memset(auxf["ones"][:, :], 1.0),
                              v.e.memset(onest[:, :], 1.0),
                              v.e.memset(hstate[:, :], 0.0),
                              v.e.memset(carry[:, :], 0.0)))
    k.phase("l2a_init2")
    k.on("scalar", lambda s: s.e.activation(Afm[:, :], Alog[:, :], AF.Exp, scale=-1.0))
    k.phase("l2a_init3")
    k.on("vector", lambda v: v.e.tensor_scalar_mul(Afm[:, :], Afm[:, :], -1.0))

    for tc in range(NCH):
        tsl = slice(tc * TC, (tc + 1) * TC)
        k.phase(f"l2a_ld{tc}")

        def ld_x(sy, tsl=tsl):
            for ki in range(8):
                sy.dma(Xc[ki][:, :], io["xT"][ki * 128:(ki + 1) * 128, tsl])
        k.on("sync", ld_x)

        k.phase(f"l2a_ln{tc}")
        emit_ln(k, f"l2a_ln{tc}", Xc, XH, TC, auxf)
        k.phase(f"l2a_cast{tc}")

        def cast_sv(v):
            for ki in range(8):
                v.e.tensor_copy(XB[ki][:, :], XH[ki][:, :])
        k.on("vector", cast_sv)

        def st_xh(g, tsl=tsl):
            for ki in range(8):
                g.dma(io["xh"][ki * 128:(ki + 1) * 128, tsl], XH[ki][:, :])
        k.on("gpsimd", st_xh)

        k.phase(f"l2a_proj{tc}")
        pj = k.sem("l2a_pj")
        pe = k.sem("l2a_pe")
        xsb = k.sem("l2a_xsb")
        vdt = k.sem("l2a_vdt")
        pj_base, pe_base = pj.cnt, pe.cnt
        xsb_done = xsb.cnt + 1
        pj.expect(26)
        pe.expect(26)
        xsb.expect(1)
        vdt_base = vdt.cnt
        vdt.expect(8)

        def proj_t(t, pj=pj, pe=pe, pe_base=pe_base, xsb=xsb, xsb_done=xsb_done):
            tg = 0
            for mo in range(16):
                if tg >= 4:
                    t.e.wait_ge(pe.h, pe_base + tg - 3)
                ins = None
                for ki in range(8):
                    ins = t.e.matmul(ps_c[tg % 4][:, :],
                                     winb[:, ki * 2048 + mo * 128: ki * 2048 + (mo + 1) * 128],
                                     XB[ki][:, :], start=(ki == 0), stop=(ki == 7))
                ins.then_inc(pj.h, 1)
                tg += 1
            t.e.wait_ge(xsb.h, xsb_done)
            for which in range(2):
                wb = wBb if which == 0 else wCb
                dst = ps_s[:, :] if which == 0 else ps_r[0:16, :]
                ins = None
                for ki in range(8):
                    ins = t.e.matmul(dst, wb[:, ki * 16:(ki + 1) * 16],
                                     XSB[ki][:, :], start=(ki == 0), stop=(ki == 7))
                ins.then_inc(pj.h, 1)
                tg += 1
            for mo in range(8):
                if tg >= 4:
                    t.e.wait_ge(pe.h, pe_base + tg - 3)
                ins = None
                for ki in range(8):
                    ins = t.e.matmul(ps_c[tg % 4][:, :],
                                     wdtb[:, ki * 1024 + mo * 128: ki * 1024 + (mo + 1) * 128],
                                     XSB[ki][:, :], start=(ki == 0), stop=(ki == 7))
                ins.then_inc(pj.h, 1)
                tg += 1

        def proj_e(s, pj=pj, pj_base=pj_base, pe=pe):
            tg = 0
            for mo in range(16):
                s.e.wait_ge(pj.h, pj_base + tg + 1)
                if mo < 8:
                    s.e.activation(XS[mo][:, :], ps_c[tg % 4][:, :], AF.Identity,
                                   bias=bx[:, mo:mo + 1]).then_inc(pe.h, 1)
                else:
                    s.e.activation(ZT[mo - 8][:, :], ps_c[tg % 4][:, :], AF.Silu,
                                   bias=bz[:, mo - 8:mo - 7]).then_inc(pe.h, 1)
                tg += 1
            s.e.wait_ge(pj.h, pj_base + 17)
            s.e.activation(BmT[:, :], ps_s[:, :], AF.Identity,
                           bias=bB[:, 0:1]).then_inc(pe.h, 1)
            s.e.wait_ge(pj.h, pj_base + 18)
            s.e.activation(CmT[:, :], ps_r[0:16, :], AF.Identity,
                           bias=bC[:, 0:1]).then_inc(pe.h, 1)
            tg += 2
            for mo in range(8):
                s.e.wait_ge(pj.h, pj_base + tg + 1)
                s.mark(s.e.activation(SPP[mo][:, :], ps_c[tg % 4][:, :], AF.Identity,
                                      bias=bdt[:, mo:mo + 1]))
                s.join()
                s.mark(s.e.activation(sp2[:, :], SPP[mo][:, :], AF.Abs))
                s.join()
                s.mark(s.e.activation(sp3[:, :], sp2[:, :], AF.Exp, scale=-1.0))
                s.join()
                s.e.activation(DT[mo][:, :], sp3[:, :], AF.Ln, bias=1.0)\
                    .then_inc(pe.h, 1)
                tg += 1

        def proj_v(v, pe=pe, pe_base=pe_base, xsb=xsb, vdt=vdt):
            v.e.wait_ge(pe.h, pe_base + 8)
            ins = None
            for ki in range(8):
                ins = v.e.tensor_copy(XSB[ki][:, :], XS[ki][:, :])
            ins.then_inc(xsb.h, 1)
            for mo in range(8):
                v.e.wait_ge(pe.h, pe_base + 18 + mo + 1)
                v.e.scalar_tensor_tensor(DT[mo][:, :], SPP[mo][:, :], 0.0,
                                         DT[mo][:, :], op0=ALU.max, op1=ALU.add)\
                    .then_inc(vdt.h, 1)
        k.on("tensor", proj_t)
        k.on("scalar", proj_e)
        k.on("vector", proj_v)

        k.phase(f"l2a_pre{tc}")

        def pre_v(v, tc=tc):
            for ki in range(8):
                v.mark(v.e.tensor_tensor(DTX[ki][:, :], DT[ki][:, :], XS[ki][:, :],
                                         op=ALU.mult))
                v.mark(v.e.tensor_tensor_scan(CS[ki][:, :], onest[:, :], DT[ki][:, :],
                                              carry[:, ki:ki + 1],
                                              op0=ALU.mult, op1=ALU.add))
            v.join()
            for ki in range(8):
                v.mark(v.e.tensor_copy(carry[:, ki:ki + 1], CS[ki][:, TC - 1:TC]))
            v.join()
        k.on("vector", pre_v)

        def st_z(g, tsl=tsl):
            for ki in range(8):
                g.dma(io["z"][ki * 128:(ki + 1) * 128, tsl], ZT[ki][:, :])
        k.on("gpsimd", st_z)

        k.phase(f"l2a_scan{tc}")
        bc = k.sem("l2a_bc")
        bce = k.sem("l2a_bce")
        ex = k.sem("l2a_ex")
        uu = k.sem("l2a_uu")
        sc = k.sem("l2a_sc")
        bc_base, bce_base, ex_base, uu_base, sc_base = \
            bc.cnt, bce.cnt, ex.cnt, uu.cnt, sc.cnt
        bc.expect(16)
        bce.expect(32)
        ex.expect(128)
        uu.expect(128)
        sc.expect(128 * 3)

        def scan_t(t, bc=bc, bce=bce, bce_base=bce_base, uu=uu, uu_base=uu_base,
                   sc=sc, sc_base=sc_base):
            for n in range(16):
                if n >= 1:
                    t.e.wait_ge(uu.h, uu_base + 8 * n)
                    t.e.wait_ge(sc.h, sc_base + 24 * n - 1)
                t.e.matmul(ps_r[:, :], SEL[:, n * 128:(n + 1) * 128], BmT[:, :],
                           start=True, stop=True)
                t.e.matmul(ps_a[:, :], SEL[:, n * 128:(n + 1) * 128], CmT[:, :],
                           start=True, stop=True).then_inc(bc.h, 1)


        def scan_ex(s, ex=ex, sc=sc, sc_base=sc_base):
            i = 0
            for n in range(16):
                for kf in range(8):
                    if i >= 1:
                        s.e.wait_ge(sc.h, sc_base + 3 * i - 2)
                    s.e.activation(abuf[:, :], DT[kf][:, :], AF.Exp,
                                   scale=Afm[:, kf * 16 + n:kf * 16 + n + 1])\
                        .then_inc(ex.h, 1)
                    i += 1

        def scan_u(g, bce=bce, bce_base=bce_base, uu=uu, sc=sc, sc_base=sc_base):
            i = 0
            for n in range(16):
                g.e.wait_ge(bce.h, bce_base + 2 * n + 1)
                for kf in range(8):
                    if i >= 1:
                        g.e.wait_ge(sc.h, sc_base + 3 * i - 2)
                    g.e.tensor_tensor(ubuf[:, :], DTX[kf][:, :], Bb[:, :],
                                      op=ALU.mult).then_inc(uu.h, 1)
                    i += 1

        def scan_v(v2, ex=ex, ex_base=ex_base, uu=uu, uu_base=uu_base, sc=sc,
                   sc_base=sc_base, bc=bc, bc_base=bc_base, bce=bce,
                   bce_base=bce_base):
            i = 0
            for n in range(16):
                v2.e.wait_ge(bc.h, bc_base + n + 1)
                if n >= 1:
                    v2.e.wait_ge(uu.h, uu_base + 8 * n)
                v2.e.tensor_copy(Bb[:, :], ps_r[:, :]).then_inc(bce.h, 1)
                v2.e.tensor_copy(Cb[:, :], ps_a[:, :]).then_inc(bce.h, 1)
                v2.e.wait_ge(bce.h, bce_base + 2 * (n + 1))
                for kf in range(8):
                    v2.e.wait_ge(ex.h, ex_base + i + 1)
                    v2.e.wait_ge(uu.h, uu_base + i + 1)
                    if i >= 1:
                        v2.e.wait_ge(sc.h, sc_base + 3 * i - 1)
                    v2.e.tensor_tensor_scan(hbuf[:, :], abuf[:, :], ubuf[:, :],
                                            hstate[:, kf * 16 + n:kf * 16 + n + 1],
                                            op0=ALU.mult, op1=ALU.add)\
                        .then_inc(sc.h, 1)
                    v2.e.wait_ge(sc.h, sc_base + 3 * i + 1)
                    v2.e.tensor_copy(hstate[:, kf * 16 + n:kf * 16 + n + 1],
                                     hbuf[:, TC - 1:TC])
                    v2.e.tensor_tensor(tbuf[:, :], hbuf[:, :], Cb[:, :],
                                       op=ALU.mult).then_inc(sc.h, 1)
                    v2.e.wait_ge(sc.h, sc_base + 3 * i + 2)
                    if n == 0:
                        v2.e.tensor_copy(YL[kf][:, :], tbuf[:, :]).then_inc(sc.h, 1)
                    else:
                        v2.e.tensor_add(YL[kf][:, :], YL[kf][:, :], tbuf[:, :])\
                            .then_inc(sc.h, 1)
                    i += 1
        k.on("tensor", scan_t)
        k.on("scalar", scan_ex)
        k.on("gpsimd", scan_u)
        k.on("vector", scan_v)

        k.phase(f"l2a_ys{tc}")

        def ys_v(v2):
            for ki in range(8):
                v2.mark(v2.e.scalar_tensor_tensor(YL[ki][:, :], XS[ki][:, :],
                                                  Dfm[:, ki:ki + 1], YL[ki][:, :],
                                                  op0=ALU.mult, op1=ALU.add))
            v2.join()
        k.on("vector", ys_v)

        def st_cs(g, tsl=tsl):
            for ki in range(8):
                g.dma(io["csdt"][ki * 128:(ki + 1) * 128, tsl], CS[ki][:, :])
            g.dma(io["CmT"][:, tsl], CmT[:, :])
        k.on("gpsimd", st_cs)

        k.phase(f"l2a_ys2{tc}")

        def st_ys(g, tsl=tsl):
            for ki in range(8):
                g.dma(io["ys"][ki * 128:(ki + 1) * 128, tsl], YL[ki][:, :])
        k.on("gpsimd", st_ys)

    k.phase("l2a_sum")

    def sum_s(s):
        for kf in range(8):
            s.e.activation(Ptot[:, kf * 16:(kf + 1) * 16],
                           Afm[:, kf * 16:(kf + 1) * 16], AF.Exp,
                           scale=carry[:, kf:kf + 1])
    k.on("scalar", sum_s)

    k.phase("l2a_sum2")

    def sum_st(g):
        g.dma(io["Ptot"][:, :], Ptot[:, :])
        g.dma(io["hend"][:, :], hstate[:, :])
    k.on("gpsimd", sum_st)


def emit_l2a_B(k, sb, ps, io):
    TC = 512
    NCH = TOK // TC
    XOR = [sb(f"bXOR{i}", [128, TC]) for i in range(8)]
    YS = [sb(f"bYS{i}", [128, TC]) for i in range(8)]
    ZT = [sb(f"bZT{i}", [128, TC]) for i in range(8)]
    CS = [sb(f"bCS{i}", [128, TC]) for i in range(8)]
    XH = [sb(f"bXH{i}", [128, TC]) for i in range(8)]
    YB = [sb(f"bYB{i}", [128, TC], BF16) for i in range(8)]
    PRE = [sb(f"bPRE{i}", [128, TC]) for i in range(8)]
    XO = [sb(f"bXO{i}", [128, TC]) for i in range(8)]
    CmT = sb("bCmT", [16, TC])
    SEL = sb("bSEL", [16, 2048])
    Afm = sb("bAfm", [128, 128])
    Alog = sb("bAlog", [128, 128])
    gP = sb("bgP", [128, 4 * 128])
    gB = sb("bgB", [128, 4 * 128])
    h0 = sb("bh0", [128, 128])
    woutb = sb("bwoutb", [128, 8 * 1024], BF16)
    bout = sb("bbout", [128, 8])
    gam = sb("bgam", [128, 8])
    bet = sb("bbet", [128, 8])
    Cb = sb("bCb", [128, TC])
    pcb = sb("bpcb", [128, TC])
    tb = sb("btb", [128, TC])
    ps_a = ps("bps_a", [128, TC])
    ps_b = ps("bps_b", [128, TC])
    ps_c = [ps(f"bps_c{i}", [128, TC]) for i in range(4)]
    auxf = ln_aux_alloc(sb, ps, TC, "bf", F32, ps_sum=ps_a, ps_sq=ps_b)

    k.phase("l2b_init")

    def init_ld(g):
        for ki in range(8):
            g.dma(woutb[:, ki * 1024:(ki + 1) * 1024],
                  io["w_outT"][ki * 128:(ki + 1) * 128, :])
    k.on("gpsimd", init_ld)

    def init_ld2(sy):
        for nm, t in (("A_fm", Alog), ("SEL", SEL), ("gP", gP), ("gB", gB),
                      ("bout", bout), ("gam", gam), ("bet", bet)):
            sy.dma(t[:, :], io[nm][:, :])
    k.on("sync", init_ld2)
    k.on("vector", lambda v: v.e.memset(auxf["ones"][:, :], 1.0))
    k.phase("l2b_init2")
    k.on("scalar", lambda s: s.e.activation(Afm[:, :], Alog[:, :], AF.Exp, scale=-1.0))
    k.phase("l2b_init3")

    def compose(v):
        v.mark(v.e.tensor_scalar_mul(Afm[:, :], Afm[:, :], -1.0))
        v.mark(v.e.memset(h0[:, :], 0.0))
        v.join()
        for j in range(4):
            v.mark(v.e.tensor_tensor(h0[:, :], h0[:, :], gP[:, j * 128:(j + 1) * 128],
                                     op=ALU.mult))
            v.join()
            v.mark(v.e.tensor_add(h0[:, :], h0[:, :], gB[:, j * 128:(j + 1) * 128]))
            v.join()
    k.on("vector", compose)

    for tc in range(NCH):
        tsl = slice(tc * TC, (tc + 1) * TC)
        k.phase(f"l2b_ld{tc}")

        def ld_x(sy, tsl=tsl):
            for ki in range(8):
                sy.dma(XOR[ki][:, :], io["xT"][ki * 128:(ki + 1) * 128, tsl])
                sy.dma(YS[ki][:, :], io["ys"][ki * 128:(ki + 1) * 128, tsl])
                sy.dma(ZT[ki][:, :], io["z"][ki * 128:(ki + 1) * 128, tsl])
                sy.dma(CS[ki][:, :], io["csdt"][ki * 128:(ki + 1) * 128, tsl])
                sy.dma(XH[ki][:, :], io["xh"][ki * 128:(ki + 1) * 128, tsl])
        k.on("sync", ld_x)

        def ld_cm(g, tsl=tsl):
            g.dma(CmT[:, :], io["CmT"][:, tsl])
        k.on("gpsimd", ld_cm)

        k.phase(f"l2b_corr{tc}")
        cb = k.sem("l2b_cb")
        cbe = k.sem("l2b_cbe")
        cex = k.sem("l2b_ex")
        cta = k.sem("l2b_ta")
        cb_base, cbe_base, cex_base, cta_base = cb.cnt, cbe.cnt, cex.cnt, cta.cnt
        cb.expect(16)
        cbe.expect(16)
        cex.expect(128)
        cta.expect(256)

        def corr_t(t, cb=cb, cbe=cbe, cta=cta, cta_base=cta_base):
            for n in range(16):
                if n >= 1:
                    t.e.wait_ge(cta.h, cta_base + 16 * n - 1)
                t.e.matmul(ps_c[0][:, :], SEL[:, n * 128:(n + 1) * 128], CmT[:, :],
                           start=True, stop=True).then_inc(cb.h, 1)


        def corr_ex(s, cbe=cbe, cex=cex, cta=cta, cta_base=cta_base):
            i = 0
            for n in range(16):
                for kf in range(8):
                    if i >= 1:
                        s.e.wait_ge(cta.h, cta_base + 2 * i - 1)
                    s.e.activation(pcb[:, :], CS[kf][:, :], AF.Exp,
                                   scale=Afm[:, kf * 16 + n:kf * 16 + n + 1])\
                        .then_inc(cex.h, 1)
                    i += 1

        def corr_v(v2, cb=cb, cb_base=cb_base, cbe=cbe, cbe_base=cbe_base,
                   cex=cex,
                   cex_base=cex_base, cta=cta, cta_base=cta_base):
            i = 0
            for n in range(16):
                v2.e.wait_ge(cb.h, cb_base + n + 1)
                v2.e.tensor_copy(Cb[:, :], ps_c[0][:, :]).then_inc(cbe.h, 1)
                v2.e.wait_ge(cbe.h, cbe_base + n + 1)
                for kf in range(8):
                    v2.e.wait_ge(cex.h, cex_base + i + 1)
                    if i >= 1:
                        v2.e.wait_ge(cta.h, cta_base + 2 * i)
                    v2.e.scalar_tensor_tensor(tb[:, :], pcb[:, :],
                                              h0[:, kf * 16 + n:kf * 16 + n + 1],
                                              Cb[:, :], op0=ALU.mult, op1=ALU.mult)\
                        .then_inc(cta.h, 1)
                    v2.e.wait_ge(cta.h, cta_base + 2 * i + 1)
                    v2.e.tensor_add(YS[kf][:, :], YS[kf][:, :], tb[:, :])\
                        .then_inc(cta.h, 1)
                    i += 1
        k.on("tensor", corr_t)
        k.on("scalar", corr_ex)
        k.on("vector", corr_v)

        k.phase(f"l2b_yz{tc}")

        def yz_v(v2):
            for ki in range(8):
                v2.mark(v2.e.tensor_tensor(YS[ki][:, :], YS[ki][:, :], ZT[ki][:, :],
                                           op=ALU.mult))
            v2.join()
            for ki in range(8):
                v2.e.tensor_copy(YB[ki][:, :], YS[ki][:, :])
        k.on("vector", yz_v)

        k.phase(f"l2b_proj{tc}")
        pj = k.sem("l2b_pj")
        pe = k.sem("l2b_pe")
        pj_base, pe_base = pj.cnt, pe.cnt
        pj.expect(8)
        pe.expect(8)

        def proj_t(t, pj=pj, pe=pe, pe_base=pe_base):
            for mo in range(8):
                if mo >= 4:
                    t.e.wait_ge(pe.h, pe_base + mo - 3)
                ins = None
                for ki in range(8):
                    ins = t.e.matmul(ps_c[mo % 4][:, :],
                                     woutb[:, ki * 1024 + mo * 128: ki * 1024 + (mo + 1) * 128],
                                     YB[ki][:, :], start=(ki == 0), stop=(ki == 7))
                ins.then_inc(pj.h, 1)

        def proj_v(v2, pj=pj, pj_base=pj_base, pe=pe):
            for mo in range(8):
                v2.e.wait_ge(pj.h, pj_base + mo + 1)
                v2.e.scalar_tensor_tensor(PRE[mo][:, :], ps_c[mo % 4][:, :],
                                          bout[:, mo:mo + 1], XH[mo][:, :],
                                          op0=ALU.add, op1=ALU.add)\
                    .then_inc(pe.h, 1)
        k.on("tensor", proj_t)
        k.on("vector", proj_v)

        k.phase(f"l2b_ln{tc}")
        emit_ln(k, f"l2b_ln{tc}", PRE, XO, TC, auxf,
                gb=(lambda ki: gam[:, ki:ki + 1], lambda ki: bet[:, ki:ki + 1]))

        k.phase(f"l2b_res{tc}")

        def resf(v2):
            for ki in range(8):
                v2.mark(v2.e.tensor_add(XO[ki][:, :], XO[ki][:, :], XOR[ki][:, :]))
            v2.join()
        k.on("vector", resf)

        k.phase(f"l2b_st{tc}")

        def st(g, tsl=tsl):
            for ki in range(8):
                g.dma(io["xoT"][ki * 128:(ki + 1) * 128, tsl], XO[ki][:, :])
        k.on("gpsimd", st)


def build_l2a_A():
    import contextlib
    nc = bass.Bass("TRN2", target_bir_lowering=False)
    io = {}
    io["xT"] = nc.dram_tensor("xT", [DM, TOK], F32, kind="ExternalInput").ap()
    for nm, shp in (("w_inT", [1024, 2048]), ("w_dtT", [1024, 1024]),
                    ("w_BT", [1024, 16]), ("w_CT", [1024, 16]),
                    ("bx", [128, 8]), ("bz", [128, 8]), ("bdt", [128, 8]),
                    ("bB", [16, 1]), ("bC", [16, 1]), ("A_fm", [128, 128]),
                    ("D_fm", [128, 8]), ("SEL", [16, 2048])):
        io[nm] = nc.dram_tensor(nm, shp, F32, kind="ExternalInput").ap()
    for nm, shp in (("xh", [DM, TOK]), ("z", [DM, TOK]), ("ys", [DM, TOK]),
                    ("csdt", [DM, TOK]), ("CmT", [16, TOK]),
                    ("Ptot", [128, 128]), ("hend", [128, 128])):
        io[nm] = nc.dram_tensor(nm, shp, F32, kind="ExternalOutput").ap()
    with contextlib.ExitStack() as stack:
        sb = lambda n, s, dt=F32: stack.enter_context(nc.sbuf_tensor(n, s, dt))
        ps = lambda n, s, dt=F32: stack.enter_context(nc.psum_tensor(n, s, dt))
        with nc.Block() as block:
            k = Kernel(nc, stack)
            emit_l2a_A(k, sb, ps, io)
            k.finish(block)
    return nc


def build_l2a_B():
    import contextlib
    nc = bass.Bass("TRN2", target_bir_lowering=False)
    io = {}
    for nm, shp in (("xT", [DM, TOK]), ("ys", [DM, TOK]), ("z", [DM, TOK]),
                    ("csdt", [DM, TOK]),
                    ("xh", [DM, TOK]), ("CmT", [16, TOK]), ("A_fm", [128, 128]),
                    ("SEL", [16, 2048]), ("gP", [128, 512]), ("gB", [128, 512]),
                    ("w_outT", [1024, 1024]), ("bout", [128, 8]),
                    ("gam", [128, 8]), ("bet", [128, 8])):
        io[nm] = nc.dram_tensor(nm, shp, F32, kind="ExternalInput").ap()
    io["xoT"] = nc.dram_tensor("xoT", [DM, TOK], F32, kind="ExternalOutput").ap()
    with contextlib.ExitStack() as stack:
        sb = lambda n, s, dt=F32: stack.enter_context(nc.sbuf_tensor(n, s, dt))
        ps = lambda n, s, dt=F32: stack.enter_context(nc.psum_tensor(n, s, dt))
        with nc.Block() as block:
            k = Kernel(nc, stack)
            emit_l2a_B(k, sb, ps, io)
            k.finish(block)
    return nc


def fm128(v):
    v = np.asarray(v, np.float32)
    n = v.shape[1]
    return np.ascontiguousarray(
        v.reshape(8, 128, n).transpose(1, 0, 2).reshape(128, 8 * n))


def sel_const():
    S = np.zeros((16, 2048), np.float32)
    for n in range(16):
        S[n, n * 128:(n + 1) * 128] = 1.0
    return S


def emit_l1a_A(k, sb, ps, io):
    T = TOK
    NCK = T // 128
    XBbig = sb("1XB", [128, 8 * T], BF16)
    XB = [XBbig[:, i * T:(i + 1) * T] for i in range(8)]
    QT = [sb(f"1QT{i}", [128, T], BF16) for i in range(8)]
    KT = [sb(f"1KT{i}", [128, T], BF16) for i in range(8)]
    Ktm = sb("1Ktm", [128, NCK * 1024], BF16)
    Vtm = sb("1Vtm", [128, NCK * 1024], BF16)
    XBf = XBbig.ap().bitcast(F32)
    NUMpp = [XBf[:, 0:T], XBf[:, T:2 * T]]
    Vf = Vtm.ap().bitcast(F32)
    Xc = [Vf[:, i * 512:(i + 1) * 512] for i in range(8)]
    Kf = Ktm.ap().bitcast(F32)
    DEN = sb("1DEN", [16, T])
    DSTG = sb("1DSTG", [1, T])
    PT = sb("1PT", [128, 128], BF16)
    KVPf = sb("1KVPf", [64, 16 * 64])
    KVPb = sb("1KVPb", [128, 16 * 64], BF16)
    KCPf = sb("1KCPf", [64, 16])
    KCPb = sb("1KCPb", [128, 16], BF16)
    UTRI = sb("1UTRI", [128, 128])
    onesc = sb("1onesc", [128, 1], BF16)
    bqkv = sb("1bqkv", [128, 16])
    bvb = sb("1bvb", [128, 1024])
    bkb = sb("1bkb", [128, 1024])
    wqkv = sb("1wqkv", [128, 8 * 1024], BF16)
    ps_a = ps("1ps_a", [128, 512])
    ps_b = ps("1ps_b", [128, 512])
    ps_r = [ps(f"1ps_r{i}", [128, 512]) for i in range(4)]
    ps_s = ps_r[0][:, 0:128]
    ps_n = ps_r[1][0:64, 0:128]
    ps_d = ps_r[2][0:1, 0:128]
    ps_kv = ps_r[3][0:64, 0:64]
    ps_kc = ps_a[0:64, 0:1]
    auxf = {"ones": sb("1lno", [128, 128]),
            "sq2": [Kf[:, 0:512], Kf[:, 512:1024]],
            "mean": Kf[:, 1024:1536], "var": Kf[:, 1536:2048],
            "rstd": Kf[:, 2048:2560], "m2": Kf[:, 2560:3072],
            "v_r": Kf[:, 3072:3584], "ps_sum": ps_a, "ps_sq": ps_b}

    k.phase("l1a_init")

    def init_ld(sy):
        sy.dma(bqkv[:, :], io["bqkv"][:, :])
        sy.dma(UTRI[:, :], io["UTRI"][:, :])
        sy.dma(bvb[:, :], io["bvb"][:, :])
        sy.dma(bkb[:, :], io["bkb"][:, :])
    k.on("sync", init_ld)
    k.on("vector", lambda v: (v.e.memset(auxf["ones"][:, :], 1.0),
                              v.e.memset(onesc[:, :], 1.0),
                              v.e.memset(KVPf[:, :], 0.0),
                              v.e.memset(KVPb[:, :], 0.0),
                              v.e.memset(KCPf[:, :], 0.0),
                              v.e.memset(KCPb[:, :], 0.0),
                              v.e.memset(PT[:, :], 0.0)))

    for tc in range(4):
        tsl = slice(tc * 512, (tc + 1) * 512)
        k.phase(f"l1a_ld{tc}")

        def ld_x(sy, tsl=tsl):
            for ki in range(8):
                sy.dma(Xc[ki][:, :], io["xT"][ki * 128:(ki + 1) * 128, tsl])
        k.on("sync", ld_x)
        k.phase(f"l1a_ln{tc}")
        emit_ln(k, f"l1a_ln{tc}", Xc, [x[:, tsl] for x in XB], 512, auxf)

    def proj_fm(tag, off, dst):
        k.phase(f"l1a_pw_{tag}")

        def ldw(g, off=off):
            for ki in range(8):
                g.dma(wqkv[:, ki * 1024:(ki + 1) * 1024],
                      io["qkv_wT"][ki * 128:(ki + 1) * 128, off:off + 1024])
        k.on("gpsimd", ldw)
        k.phase(f"l1a_pj_{tag}")
        pj = k.sem("l1a_pj")
        pe = k.sem("l1a_pe")
        pj_base, pe_base = pj.cnt, pe.cnt
        pj.expect(32)
        pe.expect(32)
        boff = off // 128 // 8

        def pj_t(t, pj=pj, pe=pe, pe_base=pe_base):
            tg = 0
            for mo in range(8):
                for ni in range(4):
                    if tg >= 4:
                        t.e.wait_ge(pe.h, pe_base + tg - 3)
                    ins = None
                    for ki in range(8):
                        ins = t.e.matmul(ps_r[tg % 4][:, :],
                                         wqkv[:, ki * 1024 + mo * 128: ki * 1024 + (mo + 1) * 128],
                                         XB[ki][:, ni * 512:(ni + 1) * 512],
                                         start=(ki == 0), stop=(ki == 7))
                    ins.then_inc(pj.h, 1)
                    tg += 1

        def pj_e(s, pj=pj, pj_base=pj_base, pe=pe, dst=dst, boff=boff):
            tg = 0
            for mo in range(8):
                for ni in range(4):
                    s.e.wait_ge(pj.h, pj_base + tg + 1)
                    s.e.activation(dst[mo][:, ni * 512:(ni + 1) * 512],
                                   ps_r[tg % 4][:, :], AF.Relu,
                                   bias=bqkv[:, boff * 8 + mo: boff * 8 + mo + 1])\
                        .then_inc(pe.h, 1)
                    tg += 1
        k.on("tensor", pj_t)
        k.on("scalar", pj_e)
        k.phase(f"l1a_eps_{tag}")

        def eps_v(v, dst=dst):
            for mo in range(8):
                v.e.tensor_scalar_add(dst[mo][:, :], dst[mo][:, :], 1e-6)
        k.on("vector", eps_v)

    proj_fm("q", 0, QT)
    proj_fm("kf", 1024, KT)

    def proj_tm(tag, off, dstt, is_k):
        k.phase(f"l1a_pw_{tag}")

        def ldw(g, off=off):
            for ki in range(8):
                g.dma(wqkv[:, ki * 1024:(ki + 1) * 1024],
                      io["qkv_wT"][ki * 128:(ki + 1) * 128, off:off + 1024])
        k.on("gpsimd", ldw)
        k.phase(f"l1a_pj_{tag}")
        pj = k.sem("l1a_pj")
        pe = k.sem("l1a_pe")
        pj_base, pe_base = pj.cnt, pe.cnt
        pj.expect(32)
        pe.expect(32)

        def pj_t(t, pj=pj, pe=pe, pe_base=pe_base):
            tg = 0
            for ck in range(16):
                for no in range(2):
                    if tg >= 4:
                        t.e.wait_ge(pe.h, pe_base + tg - 3)
                    ins = None
                    for ki in range(8):
                        ins = t.e.matmul(ps_r[tg % 4][:, :],
                                         XB[ki][:, ck * 128:(ck + 1) * 128],
                                         wqkv[:, ki * 1024 + no * 512:(ki * 1024) + (no + 1) * 512],
                                         start=(ki == 0), stop=(ki == 7))
                    ins.then_inc(pj.h, 1)
                    tg += 1

        def pj_e(v, pj=pj, pj_base=pj_base, pe=pe, dstt=dstt, is_k=is_k):
            tg = 0
            for ck in range(16):
                for no in range(2):
                    v.e.wait_ge(pj.h, pj_base + tg + 1)
                    dst = dstt[:, ck * 1024 + no * 512: ck * 1024 + (no + 1) * 512]
                    bb = (bkb if is_k else bvb)[:, no * 512:(no + 1) * 512]
                    if is_k:
                        v.mark(v.e.tensor_tensor(dst, ps_r[tg % 4][:, :], bb,
                                                 op=ALU.add))
                    else:
                        v.e.tensor_tensor(dst, ps_r[tg % 4][:, :], bb,
                                          op=ALU.add).then_inc(pe.h, 1)
                    if is_k:
                        v.join()
                        v.e.tensor_scalar(dst, dst, 0.0, 1e-6,
                                          op0=ALU.max, op1=ALU.add)\
                            .then_inc(pe.h, 1)
                    tg += 1
        k.on("tensor", pj_t)
        k.on("vector", pj_e)

    proj_tm("ktm", 1024, Ktm, True)
    proj_tm("vtm", 2048, Vtm, False)

    k.phase("l1a_attn")
    at = k.sem("l1a_at")
    ae = k.sem("l1a_ae")
    at_base, ae_base = at.cnt, ae.cnt
    dsg = k.sem("l1a_dsg")
    ndg = [k.sem("l1a_ndg0"), k.sem("l1a_ndg1")]
    dstg_dma_base = dsg.cnt
    ndg_base = [ndg[0].cnt, ndg[1].cnt]
    NH = 16
    at.expect(NH * NCK * 5)
    ae.expect(NH * NCK * 7)

    def at_t(t, at=at, ae=ae, ae_base=ae_base):
        step = 0
        for h in range(16):
            for ck in range(16):
                i = h * 16 + ck
                ko = ck * 1024 + h * 64
                if i >= 1:
                    t.e.wait_ge(ae.h, ae_base + 7 * (i - 1) + 1)
                t.e.matmul(ps_s, KT[h // 2][(h % 2) * 64:(h % 2) * 64 + 64,
                                            ck * 128:(ck + 1) * 128],
                           QT[h // 2][(h % 2) * 64:(h % 2) * 64 + 64,
                                      ck * 128:(ck + 1) * 128],
                           start=True, stop=True).then_inc(at.h, 1)
                t.e.wait_ge(ae.h, ae_base + 7 * i + 1)
                if i >= 1:
                    t.e.wait_ge(ae.h, ae_base + 7 * (i - 1) + 7)
                t.e.matmul(ps_d, onesc[:, :], PT[:, :], start=True, stop=False)
                hb = (h % 2) * 64
                t.e.matmul(ps_d, KCPb[hb:hb + 64, h:h + 1], QT[h // 2][
                    hb:hb + 64, ck * 128:(ck + 1) * 128],
                    start=False, stop=True).then_inc(at.h, 1)
                if i >= 1:
                    t.e.wait_ge(ae.h, ae_base + 7 * (i - 1) + 3)
                t.e.matmul(ps_n, Vtm[:, ko:ko + 64], PT[:, :], start=True, stop=False)
                t.e.matmul(ps_n, KVPb[hb:hb + 64, h * 64:(h + 1) * 64],
                           QT[h // 2][hb:hb + 64, ck * 128:(ck + 1) * 128],
                           start=False, stop=True).then_inc(at.h, 1)
                if i >= 1:
                    t.e.wait_ge(ae.h, ae_base + 7 * (i - 1) + 6)
                t.e.matmul(ps_kv, Ktm[:, ko:ko + 64], Vtm[:, ko:ko + 64],
                           start=True, stop=True).then_inc(at.h, 1)
                if i >= 1:
                    t.e.wait_ge(ae.h, ae_base + 7 * (i - 1) + 7)
                t.e.matmul(ps_kc, Ktm[:, ko:ko + 64], onesc[:, :],
                           start=True, stop=True).then_inc(at.h, 1)
                step += 5

    def at_v(v, at=at, at_base=at_base, ae=ae, ae_base=ae_base,
             dstg_dma_base=dstg_dma_base, dsg=dsg, ndg=ndg, ndg_base=ndg_base):
        for h in range(16):
            for ck in range(16):
                i = h * 16 + ck
                v.e.wait_ge(at.h, at_base + 5 * i + 1)
                v.e.tensor_tensor(PT[:, :], ps_s, UTRI[:, :], op=ALU.mult)\
                    .then_inc(ae.h, 1)
                v.e.wait_ge(at.h, at_base + 5 * i + 2)
                if ck == 0 and h >= 1:
                    v.e.wait_ge(dsg.h, dstg_dma_base + 16 * h)
                v.e.tensor_copy(DSTG[0:1, ck * 128:(ck + 1) * 128], ps_d)\
                    .then_inc(ae.h, 1)
                v.e.wait_ge(at.h, at_base + 5 * i + 3)
                if h % 2 == 0 and ck == 0 and h >= 4:
                    pair = h // 2
                    v.e.wait_ge(ndg[pair % 2].h,
                                ndg_base[pair % 2] + 16 * ((pair - 2) // 2 + 1))
                v.e.tensor_copy(NUMpp[(h // 2) % 2][(h % 2) * 64:(h % 2) * 64 + 64,
                                                    ck * 128:(ck + 1) * 128], ps_n)\
                    .then_inc(ae.h, 1)
                v.e.wait_ge(at.h, at_base + 5 * i + 4)
                v.e.tensor_add(KVPf[:, h * 64:(h + 1) * 64],
                               KVPf[:, h * 64:(h + 1) * 64], ps_kv)\
                    .then_inc(ae.h, 1)
                v.e.wait_ge(at.h, at_base + 5 * i + 5)
                v.e.tensor_add(KCPf[:, h:h + 1], KCPf[:, h:h + 1], ps_kc)\
                    .then_inc(ae.h, 1)
                v.e.wait_ge(ae.h, ae_base + 7 * i + 5)
                v.e.tensor_copy(KVPb[0:64, h * 64:(h + 1) * 64],
                                KVPf[:, h * 64:(h + 1) * 64])
                v.e.tensor_copy(KVPb[64:128, h * 64:(h + 1) * 64],
                                KVPf[:, h * 64:(h + 1) * 64]).then_inc(ae.h, 1)
                v.e.tensor_copy(KCPb[0:64, h:h + 1], KCPf[:, h:h + 1])
                v.e.tensor_copy(KCPb[64:128, h:h + 1], KCPf[:, h:h + 1])\
                    .then_inc(ae.h, 1)
    k.on("tensor", at_t)
    k.on("vector", at_v)

    def at_g(g, ae=ae, ae_base=ae_base, dsg=dsg, ndg=ndg):
        for h in range(16):
            g.e.wait_ge(ae.h, ae_base + 7 * (h * 16 + 15) + 2)
            g.dma(DEN[h:h + 1, :], DSTG[0:1, :], sem=dsg)
            if h % 2 == 1:
                pair = h // 2
                g.e.wait_ge(ae.h, ae_base + 7 * (h * 16 + 15) + 3)
                g.dma(io["NUM"][pair * 128:(pair + 1) * 128, :],
                      NUMpp[pair % 2], sem=ndg[pair % 2])
    k.on("gpsimd", at_g)

    k.phase("l1a_st")

    def st(g):
        for ki in range(8):
            g.dma(io["QTb"][ki * 128:(ki + 1) * 128, :], QT[ki][:, :])
        g.dma(io["DEN"][:, :], DEN[:, :])
        g.dma(io["KVT"][:, :], KVPf[:, :])
        g.dma(io["KCT"][:, :], KCPf[:, :])
    k.on("gpsimd", st)


def emit_l1a_B(k, sb, ps, io):
    T = TOK
    NUM = [sb(f"1bNUM{i}", [128, T]) for i in range(8)]
    Xc = [NUM[i // 2][:, (i % 2) * (T // 2):(i % 2 + 1) * (T // 2)] for i in range(8)]
    QT = [sb(f"1bQT{i}", [128, T], BF16) for i in range(8)]
    AOB = [sb(f"1bAOB{i}", [128, T], BF16) for i in range(8)]
    DEN = sb("1bDEN", [16, T])
    RDB = sb("1bRDB", [128, 512])
    gKV = sb("1bgKV", [64, 4 * 1024])
    gKC = sb("1bgKC", [64, 4 * 16])
    KVPb = sb("1bKVPb", [128, 1024], BF16)
    KCR = sb("1bKCR", [128, 8 * 128], BF16)
    SELH = sb("1bSELH", [16, 1024])
    woutb = sb("1bwout", [128, 8 * 1024], BF16)
    bo = sb("1bbo", [128, 8])
    ps_r = [ps(f"1bps_r{i}", [128, 512]) for i in range(4)]

    k.phase("l1b_init")

    def init_ld(sy):
        sy.dma(gKV[:, :], io["gKV"][:, :])
        sy.dma(gKC[:, :], io["gKC"][:, :])
        sy.dma(SELH[:, :], io["SELH"][:, :])
        sy.dma(bo[:, :], io["bo"][:, :])
        sy.dma(DEN[:, :], io["DEN"][:, :])
        for ki in range(8):
            sy.dma(NUM[ki][:, :], io["NUM"][ki * 128:(ki + 1) * 128, :])
    k.on("sync", init_ld)

    def init_g(g):
        for ki in range(8):
            g.dma(QT[ki][:, :], io["QTb"][ki * 128:(ki + 1) * 128, :])
        g.dma(woutb[:, :], io["out_wT"][:, :])
    k.on("gpsimd", init_g)

    k.phase("l1b_pref")

    def pref_v(v):
        v.mark(v.e.tensor_add(gKV[:, 0:1024], gKV[:, 0:1024], gKV[:, 1024:2048]))
        v.mark(v.e.tensor_add(gKV[:, 2048:3072], gKV[:, 2048:3072], gKV[:, 3072:4096]))
        v.mark(v.e.tensor_add(gKC[:, 0:16], gKC[:, 0:16], gKC[:, 16:32]))
        v.mark(v.e.tensor_add(gKC[:, 32:48], gKC[:, 32:48], gKC[:, 48:64]))
        v.join()
        v.mark(v.e.tensor_add(gKV[:, 0:1024], gKV[:, 0:1024], gKV[:, 2048:3072]))
        v.mark(v.e.tensor_add(gKC[:, 0:16], gKC[:, 0:16], gKC[:, 32:48]))
        v.join()
        v.mark(v.e.tensor_copy(KVPb[0:64, :], gKV[:, 0:1024]))
        v.mark(v.e.tensor_copy(KVPb[64:128, :], gKV[:, 0:1024]))
        v.mark(v.e.memset(KCR[:, :], 0.0))
        v.join()
        for mo in range(8):
            v.mark(v.e.tensor_copy(
                KCR[0:64, mo * 128:mo * 128 + 64],
                gKC[:, 2 * mo:2 * mo + 1].broadcast_to((64, 64))))
            v.mark(v.e.tensor_copy(
                KCR[64:128, mo * 128 + 64:mo * 128 + 128],
                gKC[:, 2 * mo + 1:2 * mo + 2].broadcast_to((64, 64))))
        v.join()
    k.on("vector", pref_v)

    k.phase("l1b_numadd")
    pa = k.sem("l1b_pa")
    pb = k.sem("l1b_pb")
    pa_base, pb_base = pa.cnt, pb.cnt
    pa.expect(16 * 4)
    pb.expect(16 * 4)

    def add_t(t, pa=pa, pb=pb, pb_base=pb_base):
        tg = 0
        for h in range(16):
            for ni in range(4):
                if tg >= 3:
                    t.e.wait_ge(pb.h, pb_base + tg - 2)
                hb = (h % 2) * 64
                t.e.matmul(ps_r[tg % 3][0:64, :],
                           KVPb[hb:hb + 64, h * 64:(h + 1) * 64],
                           QT[h // 2][hb:hb + 64, ni * 512:(ni + 1) * 512],
                           start=True, stop=True).then_inc(pa.h, 1)
                tg += 1

    def add_v(v, pa=pa, pa_base=pa_base, pb=pb):
        tg = 0
        for h in range(16):
            for ni in range(4):
                v.e.wait_ge(pa.h, pa_base + tg + 1)
                dst = NUM[h // 2][(h % 2) * 64:(h % 2) * 64 + 64,
                                  ni * 512:(ni + 1) * 512]
                v.e.tensor_add(dst, dst, ps_r[tg % 3][0:64, :]).then_inc(pb.h, 1)
                tg += 1
    k.on("tensor", add_t)
    k.on("vector", add_v)

    k.phase("l1b_div")
    dv = k.sem("l1b_dv")
    de = k.sem("l1b_de")
    dv_base, de_base = dv.cnt, de.cnt
    dv.expect(32)
    de.expect(32)

    def div_t(t, dv=dv, de=de, de_base=de_base):
        tg = 0
        for mo in range(8):
            for ni in range(4):
                if tg >= 3:
                    t.e.wait_ge(de.h, de_base + tg - 2)
                t.e.matmul(ps_r[tg % 3][:, :], SELH[:, mo * 128:(mo + 1) * 128],
                           DEN[:, ni * 512:(ni + 1) * 512], start=True, stop=False)
                t.e.matmul(ps_r[tg % 3][:, :], KCR[:, mo * 128:(mo + 1) * 128],
                           QT[mo][:, ni * 512:(ni + 1) * 512],
                           start=False, stop=True).then_inc(dv.h, 1)
                tg += 1

    def div_v(v, dv=dv, dv_base=dv_base, de=de, de_base=de_base):
        tg = 0
        for mo in range(8):
            for ni in range(4):
                v.e.wait_ge(dv.h, dv_base + tg + 1)
                if tg >= 1:
                    v.e.wait_ge(de.h, de_base + tg)
                v.mark(v.e.tensor_scalar_max(RDB[:, :], ps_r[tg % 3][:, :], 1e-6))
                v.join()
                v.mark(v.e.reciprocal(RDB[:, :], RDB[:, :]))
                v.join()
                v.e.tensor_tensor(AOB[mo][:, ni * 512:(ni + 1) * 512],
                                  NUM[mo][:, ni * 512:(ni + 1) * 512],
                                  RDB[:, :], op=ALU.mult).then_inc(de.h, 1)
                tg += 1
    k.on("tensor", div_t)
    k.on("vector", div_v)

    for tc in range(2):
        tsl = slice(tc * 1024, (tc + 1) * 1024)
        k.phase(f"l1b_ldx{tc}")

        def ldx(sy, tsl=tsl):
            for ki in range(8):
                sy.dma(Xc[ki][:, :], io["xT"][ki * 128:(ki + 1) * 128, tsl])
        k.on("sync", ldx)
        k.phase(f"l1b_proj{tc}")
        pj = k.sem("l1b_pj")
        pe = k.sem("l1b_pe")
        pj_base, pe_base = pj.cnt, pe.cnt
        pj.expect(16)
        pe.expect(16)

        def proj_t(t, pj=pj, pe=pe, pe_base=pe_base, tsl=tsl):
            tg = 0
            for mo in range(8):
                for ni in range(2):
                    if tg >= 4:
                        t.e.wait_ge(pe.h, pe_base + tg - 3)
                    ins = None
                    for ki in range(8):
                        ins = t.e.matmul(
                            ps_r[tg % 4][:, :],
                            woutb[:, ki * 1024 + mo * 128: ki * 1024 + (mo + 1) * 128],
                            AOB[ki][:, tsl][:, ni * 512:(ni + 1) * 512],
                            start=(ki == 0), stop=(ki == 7))
                    ins.then_inc(pj.h, 1)
                    tg += 1

        def proj_v(v, pj=pj, pj_base=pj_base, pe=pe):
            tg = 0
            for mo in range(8):
                for ni in range(2):
                    v.e.wait_ge(pj.h, pj_base + tg + 1)
                    dst = Xc[mo][:, ni * 512:(ni + 1) * 512]
                    v.e.scalar_tensor_tensor(dst, ps_r[tg % 4][:, :],
                                             bo[:, mo:mo + 1], dst,
                                             op0=ALU.add, op1=ALU.add)\
                        .then_inc(pe.h, 1)
                    tg += 1
        k.on("tensor", proj_t)
        k.on("vector", proj_v)
        k.phase(f"l1b_st{tc}")

        def st(g, tsl=tsl):
            for ki in range(8):
                g.dma(io["xoT"][ki * 128:(ki + 1) * 128, tsl], Xc[ki][:, :])
        k.on("gpsimd", st)


def build_l1a_A():
    import contextlib
    nc = bass.Bass("TRN2", target_bir_lowering=False)
    io = {}
    io["xT"] = nc.dram_tensor("xT", [DM, TOK], F32, kind="ExternalInput").ap()
    for nm, shp in (("qkv_wT", [1024, 3072]), ("bqkv", [128, 16]),
                    ("UTRI", [128, 128]), ("bvb", [128, 1024]), ("bkb", [128, 1024])):
        io[nm] = nc.dram_tensor(nm, shp, F32, kind="ExternalInput").ap()
    for nm, shp, dt in (("NUM", [DM, TOK], F32), ("DEN", [16, TOK], F32),
                        ("QTb", [DM, TOK], BF16), ("KVT", [64, 1024], F32),
                        ("KCT", [64, 16], F32)):
        io[nm] = nc.dram_tensor(nm, shp, dt, kind="ExternalOutput").ap()
    with contextlib.ExitStack() as stack:
        sb = lambda n, s, dt=F32: stack.enter_context(nc.sbuf_tensor(n, s, dt))
        ps = lambda n, s, dt=F32: stack.enter_context(nc.psum_tensor(n, s, dt))
        with nc.Block() as block:
            k = Kernel(nc, stack)
            emit_l1a_A(k, sb, ps, io)
            k.finish(block)
    return nc


def build_l1a_B():
    import contextlib
    nc = bass.Bass("TRN2", target_bir_lowering=False)
    io = {}
    io["xT"] = nc.dram_tensor("xT", [DM, TOK], F32, kind="ExternalInput").ap()
    for nm, shp, dt in (("NUM", [DM, TOK], F32), ("DEN", [16, TOK], F32),
                        ("QTb", [DM, TOK], BF16), ("gKV", [64, 4096], F32),
                        ("gKC", [64, 64], F32), ("SELH", [16, 1024], F32),
                        ("out_wT", [128, 8192], F32), ("bo", [128, 8], F32)):
        io[nm] = nc.dram_tensor(nm, shp, dt, kind="ExternalInput").ap()
    io["xoT"] = nc.dram_tensor("xoT", [DM, TOK], F32, kind="ExternalOutput").ap()
    with contextlib.ExitStack() as stack:
        sb = lambda n, s, dt=F32: stack.enter_context(nc.sbuf_tensor(n, s, dt))
        ps = lambda n, s, dt=F32: stack.enter_context(nc.psum_tensor(n, s, dt))
        with nc.Block() as block:
            k = Kernel(nc, stack)
            emit_l1a_B(k, sb, ps, io)
            k.finish(block)
    return nc


def emit_l3a_A(k, sb, ps, io):
    T = TOK
    Xc = [sb(f"3aXc{i}", [128, 512]) for i in range(8)]
    XB = [sb(f"3aXB{i}", [128, T], BF16) for i in range(8)]
    QT = [sb(f"3aQT{i}", [128, T], BF16) for i in range(8)]
    KTt = [sb(f"3aKT{i}", [128, T], BF16) for i in range(8)]
    Vtm = sb("3aVtm", [128, 16 * 1024], BF16)
    wq = sb("3awq", [128, 8 * 1024], BF16)
    bq = sb("3abq", [128, 8])
    bk = sb("3abk", [128, 8])
    bvb = sb("3abvb", [128, 1024])
    QSC = sb("3aQSC", [128, 512])
    ps_a = ps("3aps_a", [128, 512])
    ps_b = ps("3aps_b", [128, 512])
    ps_r = [ps(f"3aps_r{i}", [128, 512]) for i in range(4)]
    auxf = ln_aux_alloc(sb, ps, 512, "3af", F32, ps_sum=ps_a, ps_sq=ps_b)

    k.phase("l3a_init")

    def init_ld(sy):
        sy.dma(bq[:, :], io["bq"][:, :])
        sy.dma(bk[:, :], io["bk"][:, :])
        sy.dma(bvb[:, :], io["bvb"][:, :])
        sy.dma(QSC[:, :], io["QSC"][:, :])
    k.on("sync", init_ld)
    k.on("vector", lambda v: v.e.memset(auxf["ones"][:, :], 1.0))

    for tc in range(4):
        tsl = slice(tc * 512, (tc + 1) * 512)
        k.phase(f"l3a_ld{tc}")

        def ld_x(sy, tsl=tsl):
            for ki in range(8):
                sy.dma(Xc[ki][:, :], io["xT"][ki * 128:(ki + 1) * 128, tsl])
        k.on("sync", ld_x)
        k.phase(f"l3a_ln{tc}")
        emit_ln(k, f"l3a_ln{tc}", Xc, [x[:, tsl] for x in XB], 512, auxf)

    def proj_fm(tag, wnm, dst, bias, scaled):
        k.phase(f"l3a_pw_{tag}")

        def ldw(g, wnm=wnm):
            for ki in range(8):
                g.dma(wq[:, ki * 1024:(ki + 1) * 1024],
                      io[wnm][ki * 128:(ki + 1) * 128, :])
        k.on("gpsimd", ldw)
        k.phase(f"l3a_pj_{tag}")
        pj = k.sem("l3a_pj")
        pe = k.sem("l3a_pe")
        pj_base, pe_base = pj.cnt, pe.cnt
        pj.expect(32)
        pe.expect(32)

        def pj_t(t, pj=pj, pe=pe, pe_base=pe_base):
            tg = 0
            for mo in range(8):
                for ni in range(4):
                    if tg >= 4:
                        t.e.wait_ge(pe.h, pe_base + tg - 3)
                    ins = None
                    for ki in range(8):
                        ins = t.e.matmul(ps_r[tg % 4][:, :],
                                         wq[:, ki * 1024 + mo * 128: ki * 1024 + (mo + 1) * 128],
                                         XB[ki][:, ni * 512:(ni + 1) * 512],
                                         start=(ki == 0), stop=(ki == 7))
                    ins.then_inc(pj.h, 1)
                    tg += 1

        def pj_e(s, pj=pj, pj_base=pj_base, pe=pe, dst=dst, bias=bias):
            tg = 0
            for mo in range(8):
                for ni in range(4):
                    s.e.wait_ge(pj.h, pj_base + tg + 1)
                    s.e.activation(dst[mo][:, ni * 512:(ni + 1) * 512],
                                   ps_r[tg % 4][:, :], AF.Identity,
                                   bias=bias[:, mo:mo + 1]).then_inc(pe.h, 1)
                    tg += 1
        k.on("tensor", pj_t)
        k.on("scalar", pj_e)
        if scaled:
            k.phase(f"l3a_qsc_{tag}")

            def qsc_v(v, dst=dst):
                for mo in range(8):
                    for ni in range(4):
                        v.e.tensor_tensor(dst[mo][:, ni * 512:(ni + 1) * 512],
                                          dst[mo][:, ni * 512:(ni + 1) * 512],
                                          QSC[:, :], op=ALU.mult)
            k.on("vector", qsc_v)

    proj_fm("q", "WqT", QT, bq, True)
    proj_fm("k", "WkT", KTt, bk, False)

    k.phase("l3a_pw_v")

    def ldwv(g):
        for ki in range(8):
            g.dma(wq[:, ki * 1024:(ki + 1) * 1024],
                  io["WvT"][ki * 128:(ki + 1) * 128, :])
    k.on("gpsimd", ldwv)
    k.phase("l3a_pj_v")
    pj = k.sem("l3a_pj")
    pe = k.sem("l3a_pe")
    pj_base, pe_base = pj.cnt, pe.cnt
    pj.expect(32)
    pe.expect(32)

    def pjv_t(t, pj=pj, pe=pe, pe_base=pe_base):
        tg = 0
        for ck in range(16):
            for no in range(2):
                if tg >= 4:
                    t.e.wait_ge(pe.h, pe_base + tg - 3)
                ins = None
                for ki in range(8):
                    ins = t.e.matmul(ps_r[tg % 4][:, :],
                                     XB[ki][:, ck * 128:(ck + 1) * 128],
                                     wq[:, ki * 1024 + no * 512:(ki * 1024) + (no + 1) * 512],
                                     start=(ki == 0), stop=(ki == 7))
                ins.then_inc(pj.h, 1)
                tg += 1

    def pjv_e(v, pj=pj, pj_base=pj_base, pe=pe):
        tg = 0
        for ck in range(16):
            for no in range(2):
                v.e.wait_ge(pj.h, pj_base + tg + 1)
                v.e.tensor_tensor(
                    Vtm[:, ck * 1024 + no * 512: ck * 1024 + (no + 1) * 512],
                    ps_r[tg % 4][:, :], bvb[:, no * 512:(no + 1) * 512],
                    op=ALU.add).then_inc(pe.h, 1)
                tg += 1
    k.on("tensor", pjv_t)
    k.on("vector", pjv_e)

    k.phase("l3a_st")

    def st(g):
        for ki in range(8):
            g.dma(io["QdT"][ki * 128:(ki + 1) * 128, :], QT[ki][:, :])
            g.dma(io["KTo"][ki * 128:(ki + 1) * 128, :], KTt[ki][:, :])
        for ck in range(16):
            g.dma(io["Vtmo"][ck * 128:(ck + 1) * 128, :],
                  Vtm[:, ck * 1024:(ck + 1) * 1024])
    k.on("gpsimd", st)


def emit_l3a_B(k, sb, ps, io):
    T = TOK
    QT = [sb(f"3bQT{i}", [128, T], BF16) for i in range(8)]
    K4 = [sb(f"3bK4{i}", [128, 4096], BF16) for i in range(2)]
    V4 = sb("3bV4", [128, 32 * 256], BF16)
    Kd = sb("3bKd", [128, 2560], BF16)
    PB = sb("3bPB", [128, 512], BF16)
    YT = [sb(f"3bYT{i}", [128, T]) for i in range(8)]
    YB = [sb(f"3bYB{i}", [128, T], BF16) for i in range(8)]
    KSCB = sb("3bKSCB", [128, 2560])
    DMSK = sb("3bDMSK", [128, 4 * 512])
    wo = sb("3bwo", [128, 8 * 1024], BF16)
    bo = sb("3bbo", [128, 8])
    SQ = None
    RED = sb("3bRED", [128, 32])
    STS = sb("3bSTS", [1, 32])
    onesf = sb("3bonesf", [128, 1])
    ps_r = [ps(f"3bps_r{i}", [128, 512]) for i in range(4)]
    ps_o = ps("3bps_o", [128, 512])
    ps_t = ps("3bps_t", [1, 64])
    SQ = KSCB[:, 0:T]

    k.phase("l3b_init")

    def init_ld(sy):
        sy.dma(KSCB[:, :], io["KSCB"][:, :])
        sy.dma(DMSK[:, :], io["DMSK"][:, :])
        sy.dma(bo[:, :], io["bo"][:, :])
        for ki in range(8):
            sy.dma(QT[ki][:, :], io["QdT"][ki * 128:(ki + 1) * 128, :])
    k.on("sync", init_ld)

    def init_g(g):
        g.dma(wo[:, :], io["WoT"][:, :])
    k.on("gpsimd", init_g)
    k.on("vector", lambda v: v.e.memset(onesf[:, :], 1.0))

    for hg in range(4):
        k.phase(f"l3b_ldh{hg}")

        def ldh(g, hg=hg):
            for half in range(2):
                g.dma(K4[half][:, :],
                      io["K2T"][hg * 256 + half * 128: hg * 256 + (half + 1) * 128, :])
            for part in range(32):
                g.dma(V4[:, part * 256:(part + 1) * 256],
                      io["V2"][part * 128:(part + 1) * 128,
                               hg * 256:(hg + 1) * 256])
        k.on("gpsimd", ldh)

        for hh in range(4):
            h = hg * 4 + hh
            hb = (hh % 2) * 64
            kslc = K4[hh // 2]
            k.phase(f"l3b_at{h}")
            kd = k.sem("l3b_kd")
            stq = k.sem("l3b_stq")
            seq = k.sem("l3b_seq")
            pvq = k.sem("l3b_pvq")
            ovq = k.sem("l3b_ovq")
            kd_base, st_base, se_base, pv_base, ov_base = \
                kd.cnt, stq.cnt, seq.cnt, pvq.cnt, ovq.cnt
            kd.expect(4)
            stq.expect(80)
            seq.expect(80)
            pvq.expect(4)
            ovq.expect(4)


            def at_t(t, kd=kd, kd_base=kd_base, stq=stq, seq=seq, pvq=pvq,
                     ovq=ovq, se_base=se_base, ov_base=ov_base, h=h, hb=hb, hh=hh):
                i = 0
                for qc in range(4):
                    t.e.wait_ge(kd.h, kd_base + qc + 1)
                    if qc >= 1:
                        t.e.wait_ge(ovq.h, ov_base + qc)
                    ins = None
                    for kt in range(20):
                        if i >= 2:
                            t.e.wait_ge(seq.h, se_base + i - 1)
                        t.e.matmul(ps_r[i % 2][:, :],
                                   Kd[hb:hb + 64, kt * 128:(kt + 1) * 128],
                                   QT[h // 2][hb:hb + 64, qc * 512:(qc + 1) * 512],
                                   start=True, stop=True).then_inc(stq.h, 1)
                        t.e.wait_ge(seq.h, se_base + i + 1)
                        ins = t.e.matmul(ps_o[0:64, :],
                                         V4[:, (qc * 4 + kt) * 256 + hh * 64:
                                            (qc * 4 + kt) * 256 + hh * 64 + 64],
                                         PB[:, :], start=(kt == 0), stop=(kt == 19))
                        i += 1
                    ins.then_inc(pvq.h, 1)

            def at_v(v2, kd=kd, stq=stq, seq=seq, pvq=pvq, ovq=ovq,
                     st_base=st_base, pv_base=pv_base, h=h, hb=hb, kslc=kslc):
                i = 0
                for qc in range(4):
                    if qc >= 1:
                        v2.e.wait_ge(stq.h, st_base + 20 * qc)
                    v2.e.tensor_tensor(Kd[:, :],
                                       kslc[:, qc * 512:qc * 512 + 2560],
                                       KSCB[:, :], op=ALU.mult).then_inc(kd.h, 1)
                    for kt in range(20):
                        v2.e.wait_ge(stq.h, st_base + i + 1)
                        if kt >= 16:
                            v2.e.tensor_tensor(PB[:, :], ps_r[i % 2][:, :],
                                               DMSK[:, (kt - 16) * 512:(kt - 15) * 512],
                                               op=ALU.mult).then_inc(seq.h, 1)
                        else:
                            v2.e.tensor_copy(PB[:, :], ps_r[i % 2][:, :])\
                                .then_inc(seq.h, 1)
                        i += 1
                    v2.e.wait_ge(pvq.h, pv_base + qc + 1)
                    v2.e.tensor_copy(YT[h // 2][hb:hb + 64, qc * 512:(qc + 1) * 512],
                                     ps_o[0:64, :]).then_inc(ovq.h, 1)
            k.on("tensor", at_t)
            k.on("vector", at_v)

    k.phase("l3b_cast")

    def cast_v(v):
        for ki in range(8):
            v.e.tensor_copy(YB[ki][:, :], YT[ki][:, :])
    k.on("vector", cast_v)

    k.phase("l3b_proj")
    pj = k.sem("l3b_pj")
    pe = k.sem("l3b_pe")
    pj_base, pe_base = pj.cnt, pe.cnt
    pj.expect(32)
    pe.expect(32)

    def proj_t(t, pj=pj, pe=pe, pe_base=pe_base):
        tg = 0
        for mo in range(8):
            for ni in range(4):
                if tg >= 4:
                    t.e.wait_ge(pe.h, pe_base + tg - 3)
                ins = None
                for ki in range(8):
                    ins = t.e.matmul(ps_r[tg % 4][:, :],
                                     wo[:, ki * 1024 + mo * 128: ki * 1024 + (mo + 1) * 128],
                                     YB[ki][:, ni * 512:(ni + 1) * 512],
                                     start=(ki == 0), stop=(ki == 7))
                ins.then_inc(pj.h, 1)
                tg += 1

    def proj_e(s, pj=pj, pj_base=pj_base, pe=pe):
        tg = 0
        for mo in range(8):
            for ni in range(4):
                s.e.wait_ge(pj.h, pj_base + tg + 1)
                s.e.activation(YT[mo][:, ni * 512:(ni + 1) * 512],
                               ps_r[tg % 4][:, :], AF.Identity,
                               bias=bo[:, mo:mo + 1]).then_inc(pe.h, 1)
                tg += 1
    k.on("tensor", proj_t)
    k.on("scalar", proj_e)

    k.phase("l3b_stats")
    sqs = k.sem("l3b_sqs")
    sqs_base = sqs.cnt
    sqs.expect(8)
    rds = k.sem("l3b_rds")
    rds_base = rds.cnt
    rds.expect(16)

    def st_s(s, sqs=sqs, rds=rds, rds_base=rds_base):
        for mo in range(8):
            if mo >= 1:
                s.e.wait_ge(rds.h, rds_base + 2 * mo)
            s.e.activation(SQ[:, :], YT[mo][:, :], AF.Square).then_inc(sqs.h, 1)

    def st_v(v, sqs=sqs, sqs_base=sqs_base, rds=rds):
        for mo in range(8):
            v.e.reduce_sum(RED[:, mo:mo + 1], YT[mo][:, :], axis=AX.X)\
                .then_inc(rds.h, 1)
            v.e.wait_ge(sqs.h, sqs_base + mo + 1)
            v.e.reduce_sum(RED[:, 16 + mo:16 + mo + 1], SQ[:, :], axis=AX.X)\
                .then_inc(rds.h, 1)
    k.on("scalar", st_s)
    k.on("vector", st_v)

    k.phase("l3b_stats2")
    tts = k.sem("l3b_tts")
    tts.expect(1)

    def st2_t(t, tts=tts):
        ins = None
        for h in range(16):
            hb2 = (h % 2) * 64
            ins = t.e.matmul(ps_t[0:1, h:h + 1], onesf[hb2:hb2 + 64, 0:1],
                             RED[hb2:hb2 + 64, h // 2:h // 2 + 1],
                             start=True, stop=True)
            ins = t.e.matmul(ps_t[0:1, 16 + h:16 + h + 1], onesf[hb2:hb2 + 64, 0:1],
                             RED[hb2:hb2 + 64, 16 + h // 2:16 + h // 2 + 1],
                             start=True, stop=True)
        ins.then_inc(tts.h, 1)
    k.on("tensor", st2_t)
    k.phase("l3b_stats3")
    k.on("vector", lambda v: v.e.tensor_copy(STS[:, :], ps_t[0:1, 0:32]))

    k.phase("l3b_stf")

    def stf(g):
        for ki in range(8):
            g.dma(io["y"][ki * 128:(ki + 1) * 128, :], YT[ki][:, :])
        g.dma(io["stats"][:, :], STS[:, :])
    k.on("gpsimd", stf)


def emit_l3a_C(k, sb, ps, io):
    T = TOK
    Xc = [sb(f"3cXc{i}", [128, T]) for i in range(8)]
    YT = [sb(f"3cYT{i}", [128, T]) for i in range(8)]
    gst = sb("3cgst", [1, 128])
    tot = sb("3ctot", [1, 32])
    mrow = sb("3cmrow", [1, 16])
    rrow = sb("3crrow", [1, 16])
    MB = sb("3cMB", [128, 32])
    MBs = sb("3cMBs", [128, 8])
    RBs = sb("3cRBs", [128, 8])
    gam = sb("3cgam", [128, 8])
    bet = sb("3cbet", [128, 8])
    onesr = sb("3conesr", [1, 128])
    ps_b = ps("3cps_b", [128, 16])

    k.phase("l3c_init")

    def init_ld(sy):
        sy.dma(gst[:, :], io["gstats"][:, :])
        sy.dma(gam[:, :], io["gam"][:, :])
        sy.dma(bet[:, :], io["bet"][:, :])
        for ki in range(8):
            sy.dma(Xc[ki][:, :], io["xT"][ki * 128:(ki + 1) * 128, :])
            sy.dma(YT[ki][:, :], io["y"][ki * 128:(ki + 1) * 128, :])
    k.on("sync", init_ld)
    k.on("vector", lambda v: v.e.memset(onesr[:, :], 1.0))

    k.phase("l3c_stats")

    def stats_v(v):
        N = 64.0 * 8192.0
        v.mark(v.e.tensor_add(gst[0:1, 0:32], gst[0:1, 0:32], gst[0:1, 32:64]))
        v.join()
        v.mark(v.e.tensor_add(gst[0:1, 0:32], gst[0:1, 0:32], gst[0:1, 64:96]))
        v.join()
        v.mark(v.e.tensor_add(tot[0:1, :], gst[0:1, 0:32], gst[0:1, 96:128]))
        v.join()
        v.mark(v.e.tensor_scalar_mul(mrow[0:1, :], tot[0:1, 0:16], 1.0 / N))
        v.mark(v.e.tensor_scalar_mul(rrow[0:1, :], tot[0:1, 16:32], 1.0 / N))
        v.join()
        v.mark(v.e.tensor_tensor(tot[0:1, 0:16], mrow[0:1, :], mrow[0:1, :],
                                 op=ALU.mult))
        v.join()
        v.mark(v.e.tensor_sub(rrow[0:1, :], rrow[0:1, :], tot[0:1, 0:16]))
        v.join()
        v.mark(v.e.tensor_scalar_add(rrow[0:1, :], rrow[0:1, :], 1e-5))
        v.join()
        v.mark(v.e.reciprocal(rrow[0:1, :], rrow[0:1, :]))
        v.join()
    k.on("vector", stats_v)
    k.phase("l3c_sqrt")
    k.on("scalar", lambda s: s.e.activation(rrow[0:1, :], rrow[0:1, :], AF.Sqrt))
    k.phase("l3c_bc")
    k.on("tensor", lambda t: t.e.matmul(ps_b[:, 0:16], onesr[:, :], mrow[0:1, :],
                                        start=True, stop=True))
    k.phase("l3c_bc1e")
    k.on("vector", lambda v: v.e.tensor_copy(MB[:, 0:16], ps_b[:, 0:16]))
    k.phase("l3c_bc2")
    k.on("tensor", lambda t: t.e.matmul(ps_b[:, 0:16], onesr[:, :], rrow[0:1, :],
                                        start=True, stop=True))
    k.phase("l3c_sel")

    def sel_v(v):
        v.mark(v.e.tensor_copy(MB[:, 16:32], ps_b[:, 0:16]))
        v.join()
        for mo in range(8):
            v.mark(v.e.tensor_copy(MBs[0:64, mo:mo + 1], MB[0:64, 2 * mo:2 * mo + 1]))
            v.mark(v.e.tensor_copy(MBs[64:128, mo:mo + 1],
                                   MB[64:128, 2 * mo + 1:2 * mo + 2]))
            v.mark(v.e.tensor_copy(RBs[0:64, mo:mo + 1],
                                   MB[0:64, 16 + 2 * mo:16 + 2 * mo + 1]))
            v.mark(v.e.tensor_copy(RBs[64:128, mo:mo + 1],
                                   MB[64:128, 16 + 2 * mo + 1:16 + 2 * mo + 2]))
        v.join()
    k.on("vector", sel_v)

    k.phase("l3c_norm")

    def norm_v(v):
        for mo in range(8):
            v.mark(v.e.tensor_scalar(YT[mo][:, :], YT[mo][:, :],
                                     MBs[:, mo:mo + 1], RBs[:, mo:mo + 1],
                                     op0=ALU.subtract, op1=ALU.mult))
        v.join()
        for mo in range(8):
            v.mark(v.e.tensor_scalar(YT[mo][:, :], YT[mo][:, :],
                                     gam[:, mo:mo + 1], bet[:, mo:mo + 1],
                                     op0=ALU.mult, op1=ALU.add))
        v.join()
        for mo in range(8):
            v.mark(v.e.tensor_add(Xc[mo][:, :], Xc[mo][:, :], YT[mo][:, :]))
        v.join()
    k.on("vector", norm_v)

    k.phase("l3c_st")

    def st(g):
        for ki in range(8):
            g.dma(io["xoT"][ki * 128:(ki + 1) * 128, :], Xc[ki][:, :])
    k.on("gpsimd", st)


def build_l3a_A():
    import contextlib
    nc = bass.Bass("TRN2", target_bir_lowering=False)
    io = {}
    io["xT"] = nc.dram_tensor("xT", [DM, TOK], F32, kind="ExternalInput").ap()
    for nm, shp in (("WqT", [1024, 1024]), ("WkT", [1024, 1024]),
                    ("WvT", [1024, 1024]), ("bq", [128, 8]), ("bk", [128, 8]),
                    ("bvb", [128, 1024]), ("QSC", [128, 512])):
        io[nm] = nc.dram_tensor(nm, shp, F32, kind="ExternalInput").ap()
    for nm, shp in (("QdT", [DM, TOK]), ("KTo", [DM, TOK])):
        io[nm] = nc.dram_tensor(nm, shp, BF16, kind="ExternalOutput").ap()
    io["Vtmo"] = nc.dram_tensor("Vtmo", [TOK, DM], BF16, kind="ExternalOutput").ap()
    with contextlib.ExitStack() as stack:
        sb = lambda n, s, dt=F32: stack.enter_context(nc.sbuf_tensor(n, s, dt))
        ps = lambda n, s, dt=F32: stack.enter_context(nc.psum_tensor(n, s, dt))
        with nc.Block() as block:
            k = Kernel(nc, stack)
            emit_l3a_A(k, sb, ps, io)
            k.finish(block)
    return nc


def build_l3a_B():
    import contextlib
    nc = bass.Bass("TRN2", target_bir_lowering=False)
    io = {}
    for nm, shp, dt in (("QdT", [DM, TOK], BF16), ("K2T", [DM, 2 * TOK], BF16),
                        ("V2", [2 * TOK, DM], BF16), ("KSCB", [128, 2560], F32),
                        ("DMSK", [128, 2048], F32), ("WoT", [128, 8192], F32),
                        ("bo", [128, 8], F32)):
        io[nm] = nc.dram_tensor(nm, shp, dt, kind="ExternalInput").ap()
    io["y"] = nc.dram_tensor("y", [DM, TOK], F32, kind="ExternalOutput").ap()
    io["stats"] = nc.dram_tensor("stats", [1, 32], F32, kind="ExternalOutput").ap()
    with contextlib.ExitStack() as stack:
        sb = lambda n, s, dt=F32: stack.enter_context(nc.sbuf_tensor(n, s, dt))
        ps = lambda n, s, dt=F32: stack.enter_context(nc.psum_tensor(n, s, dt))
        with nc.Block() as block:
            k = Kernel(nc, stack)
            emit_l3a_B(k, sb, ps, io)
            k.finish(block)
    return nc


def build_l3a_C():
    import contextlib
    nc = bass.Bass("TRN2", target_bir_lowering=False)
    io = {}
    for nm, shp in (("xT", [DM, TOK]), ("y", [DM, TOK]), ("gstats", [1, 128]),
                    ("gam", [128, 8]), ("bet", [128, 8])):
        io[nm] = nc.dram_tensor(nm, shp, F32, kind="ExternalInput").ap()
    io["xoT"] = nc.dram_tensor("xoT", [DM, TOK], F32, kind="ExternalOutput").ap()
    with contextlib.ExitStack() as stack:
        sb = lambda n, s, dt=F32: stack.enter_context(nc.sbuf_tensor(n, s, dt))
        ps = lambda n, s, dt=F32: stack.enter_context(nc.psum_tensor(n, s, dt))
        with nc.Block() as block:
            k = Kernel(nc, stack)
            emit_l3a_C(k, sb, ps, io)
            k.finish(block)
    return nc


def ap3(a, groups, inner):
    import dataclasses
    p = list(a.ap)
    assert p[-1][0] == 1 and p[-1][1] == groups * inner
    return dataclasses.replace(a, ap=[list(p[0]), [inner, groups], [1, inner]])


def emit_l0a_A(k, sb, ps, io, pools=True, dov=True):
    T = TOK
    XBbig = sb("0aXB", [128, 8 * T], BF16)
    XB = [XBbig[:, i * T:(i + 1) * T] for i in range(8)]
    Xc = [sb(f"0aXc{i}", [128, 512]) for i in range(8)]
    wbuf = sb("0awb", [128, 8 * 1024], BF16)
    SSB = sb("0aSSB", [128, 2 * 512], BF16)
    QP = sb("0aQP", [128, 512])
    KP = sb("0aKP", [128, 512])
    PSF = sb("0aPSF", [128, 2 * 512])
    TB = sb("0aTB", [128, 2 * 256])
    ps_r = [ps(f"0aps{i}", [128, 512]) for i in range(4)]
    auxf = {"ones": sb("0alno", [128, 128]),
            "sq2": [sb("0asq0", [128, 512]), sb("0asq1", [128, 512])],
            "mean": sb("0amean", [128, 512]), "var": sb("0avar", [128, 512]),
            "rstd": sb("0arstd", [128, 512]), "m2": sb("0am2", [128, 512]),
            "v_r": sb("0avr", [128, 512]),
            "ps_sum": ps_r[0], "ps_sq": ps_r[1]}

    k.phase("l0a_init")
    k.on("vector", lambda v: v.e.memset(auxf["ones"][:, :], 1.0))

    for tc in range(4):
        tsl = slice(tc * 512, (tc + 1) * 512)
        k.phase(f"l0a_ld{tc}")

        def ld_x(sy, tsl=tsl):
            for ki in range(8):
                sy.dma(Xc[ki][:, :], io["xT"][ki * 128:(ki + 1) * 128, tsl])
        k.on("sync", ld_x)
        k.phase(f"l0a_ln{tc}")
        emit_ln(k, f"l0a_ln{tc}", Xc, [x[:, tsl] for x in XB], 512, auxf)

    def proj_fm(tag, w_key, out_key, pool_dst):
        k.phase(f"l0a_w_{tag}")

        def ldw(g, w_key=w_key):
            for ki in range(8):
                g.dma(wbuf[:, ki * 1024:(ki + 1) * 1024],
                      io[w_key][ki * 128:(ki + 1) * 128, :])
        k.on("gpsimd", ldw)
        k.phase(f"l0a_p_{tag}")
        pj = k.sem("l0a_pj")
        se = k.sem("l0a_se")
        vstg = k.sem("l0a_vstg")
        vtd = k.sem("l0a_vtd")
        sd = [k.sem("l0a_sd0"), k.sem("l0a_sd1")]
        pj_base, se_base = pj.cnt, se.cnt
        vstg_base, vtd_base = vstg.cnt, vtd.cnt
        sd_base = [sd[0].cnt, sd[1].cnt]
        pj.expect(32)
        se.expect(32)
        if pool_dst is not None:
            vstg.expect(32)
            vtd.expect(32)
        for tg in range(32):
            sd[tg % 2].expect(16)

        def pj_t(t, pj=pj, se=se, vstg=vstg, se_base=se_base,
                 vstg_base=vstg_base, pool_dst=pool_dst):
            tg = 0
            for mo in range(8):
                for ni in range(4):
                    if tg >= 4:
                        t.e.wait_ge(se.h, se_base + tg - 3)
                        if pool_dst is not None:
                            t.e.wait_ge(vstg.h, vstg_base + tg - 3)
                    ins = None
                    for ki in range(8):
                        ins = t.e.matmul(
                            ps_r[tg % 4][:, :],
                            wbuf[:, ki * 1024 + mo * 128:ki * 1024 + (mo + 1) * 128],
                            XB[ki][:, ni * 512:(ni + 1) * 512],
                            start=(ki == 0), stop=(ki == 7))
                    ins.then_inc(pj.h, 1)
                    tg += 1

        def pj_s(s, pj=pj, pj_base=pj_base, se=se, sd=sd, sd_base=sd_base,
                 vstg=vstg, vtd=vtd, vtd_base=vtd_base, pool_dst=pool_dst):
            tg = 0
            for mo in range(8):
                for ni in range(4):
                    s.e.wait_ge(pj.h, pj_base + tg + 1)
                    s.e.wait_ge(sd[tg % 2].h, sd_base[tg % 2] + 16 * (tg // 2))
                    s.e.activation(SSB[:, (tg % 2) * 512:(tg % 2 + 1) * 512],
                                   ps_r[tg % 4][:, :], AF.Identity)\
                        .then_inc(se.h, 1)
                    if pool_dst is not None:
                        if tg >= 2:
                            s.e.wait_ge(vtd.h, vtd_base + tg - 1)
                        s.e.activation(PSF[:, (tg % 2) * 512:(tg % 2 + 1) * 512],
                                       ps_r[tg % 4][:, :], AF.Identity)\
                            .then_inc(vstg.h, 1)
                    tg += 1

        def pj_g(g, se=se, se_base=se_base, sd=sd, out_key=out_key):
            tg = 0
            for mo in range(8):
                for ni in range(4):
                    g.e.wait_ge(se.h, se_base + tg + 1)
                    g.e.dma_start(
                        out=io[out_key][mo * 128:(mo + 1) * 128,
                                        ni * 512:(ni + 1) * 512],
                        in_=SSB[:, (tg % 2) * 512:(tg % 2 + 1) * 512],
                    ).then_inc(sd[tg % 2].h, 16)
                    tg += 1

        def pj_v(v, vstg=vstg, vstg_base=vstg_base, vtd=vtd,
                 pool_dst=pool_dst):
            tg = 0
            for mo in range(8):
                for ni in range(4):
                    v.e.wait_ge(vstg.h, vstg_base + tg + 1)
                    dst = pool_dst[:, mo * 64 + ni * 16:mo * 64 + (ni + 1) * 16]
                    p = tg % 2
                    a = PSF[:, p * 512:(p + 1) * 512]
                    b = TB[:, p * 256:(p + 1) * 256]
                    w = 512
                    for _ in range(4):
                        w //= 2
                        v.mark(v.e.tensor_add(b[:, 0:w], a[:, 0:2 * w:2],
                                              a[:, 1:2 * w:2]))
                        v.join()
                        a, b = b, a
                    v.e.tensor_add(dst, a[:, 0:32:2], a[:, 1:32:2])\
                        .then_inc(vtd.h, 1)
                    tg += 1
        k.on("tensor", pj_t)
        k.on("scalar", pj_s)
        k.on("gpsimd", pj_g)
        if pool_dst is not None:
            k.on("vector", pj_v)

    proj_fm("q", "wq", "qT", QP if pools else None)
    proj_fm("k", "wk", "kT", KP if pools else None)

    if not dov:
        k.phase("l0a_stp")

        def stp0(g):
            g.dma(io["qp"][:, :], QP[:, :])
            g.dma(io["kp"][:, :], KP[:, :])
        k.on("gpsimd", stp0)
        return
    k.phase("l0a_w_v")

    def ldwv(g):
        for ki in range(8):
            g.dma(wbuf[:, ki * 1024:(ki + 1) * 1024],
                  io["wv"][ki * 128:(ki + 1) * 128, :])
    k.on("gpsimd", ldwv)
    k.phase("l0a_p_v")
    pj = k.sem("l0a_pj")
    se = k.sem("l0a_se")
    sd = [k.sem("l0a_sd0"), k.sem("l0a_sd1")]
    pj_base, se_base = pj.cnt, se.cnt
    sd_base = [sd[0].cnt, sd[1].cnt]
    pj.expect(32)
    se.expect(32)
    for tg in range(32):
        sd[tg % 2].expect(16)

    def pv_t(t, pj=pj, se=se, se_base=se_base):
        tg = 0
        for ck in range(16):
            for no in range(2):
                if tg >= 4:
                    t.e.wait_ge(se.h, se_base + tg - 3)
                ins = None
                for ki in range(8):
                    ins = t.e.matmul(
                        ps_r[tg % 4][:, :],
                        XB[ki][:, ck * 128:(ck + 1) * 128],
                        wbuf[:, ki * 1024 + no * 512:ki * 1024 + (no + 1) * 512],
                        start=(ki == 0), stop=(ki == 7))
                ins.then_inc(pj.h, 1)
                tg += 1

    def pv_s(s, pj=pj, pj_base=pj_base, se=se, sd=sd, sd_base=sd_base):
        tg = 0
        for ck in range(16):
            for no in range(2):
                s.e.wait_ge(pj.h, pj_base + tg + 1)
                s.e.wait_ge(sd[tg % 2].h, sd_base[tg % 2] + 16 * (tg // 2))
                s.e.activation(SSB[:, (tg % 2) * 512:(tg % 2 + 1) * 512],
                               ps_r[tg % 4][:, :], AF.Identity)\
                    .then_inc(se.h, 1)
                tg += 1

    def pv_g(g, se=se, se_base=se_base, sd=sd):
        tg = 0
        for ck in range(16):
            for no in range(2):
                g.e.wait_ge(se.h, se_base + tg + 1)
                g.e.dma_start(
                    out=io["vtm"][ck * 128:(ck + 1) * 128,
                                  no * 512:(no + 1) * 512],
                    in_=SSB[:, (tg % 2) * 512:(tg % 2 + 1) * 512],
                ).then_inc(sd[tg % 2].h, 16)
                tg += 1
    k.on("tensor", pv_t)
    k.on("scalar", pv_s)
    k.on("gpsimd", pv_g)

    k.phase("l0a_stp")

    def stp(g):
        g.dma(io["qp"][:, :], QP[:, :])
        g.dma(io["kp"][:, :], KP[:, :])
    k.on("gpsimd", stp)


def emit_l0a_B(k, sb, ps, io):
    QPA = sb("0bQPA", [128, 4096])
    KPA = sb("0bKPA", [128, 4096])
    SC = sb("0bSC", [128, 2 * 256])
    SCW = sb("0bSCW", [128, 2 * 256])
    M = sb("0bM", [128, 2])
    EQ = sb("0bEQ", [128, 256])
    MSK = sb("0bMSK", [128, 2 * 256], BF16)
    ps_r = [ps(f"0bps{i}", [128, 256]) for i in range(2)]

    k.phase("l0b_ld")
    k.on("sync", lambda sy: (sy.dma(QPA[:, :], io["qp_all"][:, :]),
                             sy.dma(KPA[:, :], io["kp_all"][:, :])))
    k.phase("l0b_sc")
    scs = k.sem("l0b_scs")
    scs_base = scs.cnt
    scs.expect(2)

    def sc_t(t, scs=scs):
        for ih in range(2):
            ins = None
            for b in range(2):
                for hp in range(8):
                    g = b * 8 + hp
                    ins = t.e.matmul(
                        ps_r[ih][:, :],
                        QPA[:, g * 256 + ih * 128:g * 256 + (ih + 1) * 128],
                        KPA[:, g * 256:(g + 1) * 256],
                        start=(g == 0), stop=(g == 15))
            ins.then_inc(scs.h, 1)

    def sc_v(v, scs=scs, scs_base=scs_base):
        for ih in range(2):
            v.e.wait_ge(scs.h, scs_base + ih + 1)
            v.mark(v.e.tensor_copy(SC[:, ih * 256:(ih + 1) * 256], ps_r[ih][:, :]))
            v.join()
            v.mark(v.e.tensor_copy(SCW[:, ih * 256:(ih + 1) * 256],
                                   ps_r[ih][:, :]))
            v.join()
    k.on("tensor", sc_t)
    k.on("vector", sc_v)

    k.phase("l0b_top")

    def top_v(v):
        for ih in range(2):
            W = SCW[:, ih * 256:(ih + 1) * 256]
            for r in range(51):
                v.mark(v.e.reduce_max(M[:, ih:ih + 1], W, axis=AX.X))
                v.join()
                if r < 50:
                    v.mark(v.e.tensor_scalar(EQ[:, :], W, M[:, ih:ih + 1],
                                             -1e9, op0=ALU.is_ge, op1=ALU.mult))
                    v.join()
                    v.mark(v.e.tensor_add(W, W, EQ[:, :]))
                    v.join()
            v.mark(v.e.tensor_scalar(MSK[:, ih * 256:(ih + 1) * 256],
                                     SC[:, ih * 256:(ih + 1) * 256],
                                     M[:, ih:ih + 1], 240.0,
                                     op0=ALU.is_ge, op1=ALU.mult))
            v.join()
    k.on("vector", top_v)

    k.phase("l0b_st")
    k.on("gpsimd", lambda g: g.dma(io["msk"][:, :], MSK[:, :]))


def emit_l0a_C(k, sb, ps, io, nh=16):
    T = TOK
    KST = [sb(f"0cKST{i}", [128, 8192], BF16) for i in range(2)]
    QSTK = [sb(f"0cQS{i}", [128, T], BF16) for i in range(2)]
    VA = [sb(f"0cVA{i}", [128, 64 * 128], BF16) for i in range(2)]
    OACC = sb("0cOACC", [128, 8 * T], BF16)
    WOB = sb("0cWOB", [128, 8 * 1024], BF16)
    PTS = sb("0cPTS", [128, 2 * 512], BF16)
    DSTG = sb("0cDSTG", [128, 512])
    DRH = sb("0cDRH", [1, 512])
    RB = sb("0cRB", [64, 512])
    ONE1 = sb("0cONE1", [1, 64])
    NEG30 = sb("0cNEG30", [128, 1])
    XRS = sb("0cXRS", [128, 2 * 512])
    XOS = sb("0cXOS", [128, 2 * 512])
    ps_s = [ps(f"0cpss{i}", [128, 512]) for i in range(3)]
    ps_pv = [ps(f"0cpsv{i}", [128, 512]) for i in range(2)]
    ps_db = ps("0cpsdb", [128, 512])
    ps_r = [ps_s[0], ps_s[1], ps_pv[0], ps_pv[1]]

    k.phase("l0c_init")

    def init_ld(sy):
        for i in range(2):
            sy.dma(KST[i][64:128, :], io["mi240"][:, :])
            sy.dma(QSTK[i][64:128, :], io["exq"][:, :])

    def init_g(g):
        for ki in range(8):
            g.dma(WOB[:, ki * 1024:(ki + 1) * 1024],
                  io["wo"][ki * 128:(ki + 1) * 128, :])

    def init_v(v):
        v.e.memset(OACC[:, :], 0.0)
        v.e.memset(ONE1[:, :], 1.0)
        v.e.memset(NEG30[:, :], -30.0)
        for i in range(2):
            v.mark(v.e.memset(VA[i][:, :], 0.0))
        v.join()
        for i in range(2):
            for kt in range(64):
                v.e.memset(VA[i][:, kt * 128 + 64:kt * 128 + 65], 1.0)
    k.on("sync", init_ld)
    k.on("gpsimd", init_g)
    k.on("vector", init_v)

    k.phase("l0c_attn")
    ld = [k.sem("l0c_ld0"), k.sem("l0c_ld1")]
    st = k.sem("l0c_st")
    ev = k.sem("l0c_ev")
    pv = k.sem("l0c_pv")
    dsg = k.sem("l0c_dsg")
    drh = k.sem("l0c_drh")
    db = k.sem("l0c_db")
    vr = k.sem("l0c_vr")
    vd = k.sem("l0c_vd")
    ld_base = [ld[0].cnt, ld[1].cnt]
    st_base, ev_base, pv_base = st.cnt, ev.cnt, pv.cnt
    dsg_base, drh_base, db_base = dsg.cnt, drh.cnt, db.cnt
    vr_base, vd_base = vr.cnt, vd.cnt
    for h in range(nh):
        ld[h % 2].expect(66 * 16)
    NQ = nh * 4
    st.expect(NQ * 64)
    ev.expect(NQ * 64)
    pv.expect(NQ * 64)
    dsg.expect(NQ)
    drh.expect(NQ * 16)
    db.expect(NQ)
    vr.expect(NQ)
    vd.expect(NQ)

    def at_g(g, ld=ld, db=db, db_base=db_base):
        for h in range(nh):
            b = h % 2
            if h >= 2:
                g.e.wait_ge(db.h, db_base + 4 * (h - 1))
            g.e.dma_start(out=KST[b][0:64, :],
                          in_=io["kTb"][h * 64:(h + 1) * 64, :])\
                .then_inc(ld[b].h, 16)
            g.e.dma_start(out=QSTK[b][0:64, :],
                          in_=io["qT"][h * 64:(h + 1) * 64, :])\
                .then_inc(ld[b].h, 16)
            for kt in range(64):
                g.e.dma_start(out=VA[b][:, kt * 128:kt * 128 + 64],
                              in_=io["vtmb"][kt * 128:(kt + 1) * 128,
                                             h * 64:(h + 1) * 64])\
                    .then_inc(ld[b].h, 16)

    def at_t(t, ld=ld, ld_base=ld_base, st=st, ev=ev, ev_base=ev_base,
             pv=pv, drh=drh, drh_base=drh_base, db=db, vr=vr, vr_base=vr_base,
             vd=vd, vd_base=vd_base):
        for h in range(nh):
            b = h % 2
            t.e.wait_ge(ld[b].h, ld_base[b] + 66 * 16 * (h // 2 + 1))
            for qc in range(4):
                n = h * 4 + qc

                def pv_emit(kt, n=n, b=b):
                    ti = n * 64 + kt
                    t.e.wait_ge(ev.h, ev_base + ti + 1)
                    if kt == 0 and n >= 2:
                        t.e.wait_ge(vd.h, vd_base + n - 1)
                    t.e.matmul(ps_pv[n % 2][:, :],
                               VA[b][:, kt * 128:(kt + 1) * 128],
                               PTS[:, (ti % 2) * 512:(ti % 2 + 1) * 512],
                               start=(kt == 0), stop=(kt == 63))\
                        .then_inc(pv.h, 1)

                for kt in range(64):
                    ti = n * 64 + kt
                    if ti >= 3:
                        t.e.wait_ge(ev.h, ev_base + ti - 2)
                    t.e.matmul(ps_s[ti % 3][:, :],
                               KST[b][:, kt * 128:(kt + 1) * 128],
                               QSTK[b][:, qc * 512:(qc + 1) * 512],
                               start=True, stop=True).then_inc(st.h, 1)
                    if kt >= 1:
                        pv_emit(kt - 1)
                pv_emit(63)
                t.e.wait_ge(drh.h, drh_base + 16 * (n + 1))
                if n >= 1:
                    t.e.wait_ge(vr.h, vr_base + n)
                t.e.matmul(ps_db[0:64, :], ONE1[:, :], DRH[:, :],
                           start=True, stop=True).then_inc(db.h, 1)

    def at_s(s, st=st, st_base=st_base, ev=ev, pv=pv, pv_base=pv_base,
             dsg=dsg, drh=drh, drh_base=drh_base):
        for h in range(nh):
            for qc in range(4):
                n = h * 4 + qc
                for kt in range(64):
                    ti = n * 64 + kt
                    s.e.wait_ge(st.h, st_base + ti + 1)
                    if ti >= 2:
                        s.e.wait_ge(pv.h, pv_base + ti - 1)
                    s.e.activation(PTS[:, (ti % 2) * 512:(ti % 2 + 1) * 512],
                                   ps_s[ti % 3][:, :], AF.Exp,
                                   scale=0.125, bias=NEG30[:, 0:1])\
                        .then_inc(ev.h, 1)
                s.e.wait_ge(pv.h, pv_base + (n + 1) * 64)
                if n >= 1:
                    s.e.wait_ge(drh.h, drh_base + 16 * n)
                s.e.activation(DSTG[64:65, :], ps_pv[n % 2][64:65, :],
                               AF.Identity).then_inc(dsg.h, 1)

    def at_sy(sy, dsg=dsg, dsg_base=dsg_base, drh=drh, db=db, db_base=db_base):
        for n in range(NQ):
            sy.e.wait_ge(dsg.h, dsg_base + n + 1)
            if n >= 1:
                sy.e.wait_ge(db.h, db_base + n)
            sy.e.dma_start(out=DRH[:, :], in_=DSTG[64:65, :])\
                .then_inc(drh.h, 16)

    def at_v(v, db=db, db_base=db_base, vr=vr, vr_base=vr_base, vd=vd,
             vd_base=vd_base):
        for h in range(nh):
            hp, hh = h // 2, h % 2
            for qc in range(4):
                n = h * 4 + qc
                v.e.wait_ge(db.h, db_base + n + 1)
                if n >= 1:
                    v.e.wait_ge(vd.h, vd_base + n)
                v.e.reciprocal(RB[:, :], ps_db[0:64, :]).then_inc(vr.h, 1)
                v.e.wait_ge(vr.h, vr_base + n + 1)
                v.e.tensor_tensor(
                    OACC[hh * 64:hh * 64 + 64,
                         hp * T + qc * 512:hp * T + (qc + 1) * 512],
                    ps_pv[n % 2][0:64, :], RB[:, :], op=ALU.mult)\
                    .then_inc(vd.h, 1)
    k.on("gpsimd", at_g)
    k.on("tensor", at_t)
    k.on("scalar", at_s)
    k.on("sync", at_sy)
    k.on("vector", at_v)

    k.phase("l0c_proj")
    xl = [k.sem("l0c_xl0"), k.sem("l0c_xl1")]
    so = [k.sem("l0c_so0"), k.sem("l0c_so1")]
    pj = k.sem("l0c_pj")
    pe = k.sem("l0c_pe")
    xl_base = [xl[0].cnt, xl[1].cnt]
    so_base = [so[0].cnt, so[1].cnt]
    pj_base, pe_base = pj.cnt, pe.cnt
    for tg in range(32):
        xl[tg % 2].expect(16)
        so[tg % 2].expect(16)
    pj.expect(32)
    pe.expect(32)

    def pr_t(t, pj=pj, pe=pe, pe_base=pe_base):
        tg = 0
        for qc in range(4):
            for fg in range(8):
                if tg >= 4:
                    t.e.wait_ge(pe.h, pe_base + tg - 3)
                ins = None
                for hp in range(8):
                    ins = t.e.matmul(
                        ps_r[tg % 4][:, :],
                        WOB[:, hp * 1024 + fg * 128:hp * 1024 + (fg + 1) * 128],
                        OACC[:, hp * T + qc * 512:hp * T + (qc + 1) * 512],
                        start=(hp == 0), stop=(hp == 7))
                ins.then_inc(pj.h, 1)
                tg += 1

    def pr_sy(sy, pe=pe, pe_base=pe_base, xl=xl):
        tg = 0
        for qc in range(4):
            for fg in range(8):
                if tg >= 2:
                    sy.e.wait_ge(pe.h, pe_base + tg - 1)
                sy.e.dma_start(out=XRS[:, (tg % 2) * 512:(tg % 2 + 1) * 512],
                               in_=io["xT"][fg * 128:(fg + 1) * 128,
                                            qc * 512:(qc + 1) * 512])\
                    .then_inc(xl[tg % 2].h, 16)
                tg += 1

    def pr_v(v, pj=pj, pj_base=pj_base, pe=pe, xl=xl, xl_base=xl_base,
             so=so, so_base=so_base):
        tg = 0
        for qc in range(4):
            for fg in range(8):
                v.e.wait_ge(pj.h, pj_base + tg + 1)
                v.e.wait_ge(xl[tg % 2].h, xl_base[tg % 2] + 16 * (tg // 2 + 1))
                if tg >= 2:
                    v.e.wait_ge(so[tg % 2].h, so_base[tg % 2] + 16 * (tg // 2))
                v.e.tensor_tensor(XOS[:, (tg % 2) * 512:(tg % 2 + 1) * 512],
                                  ps_r[tg % 4][:, :],
                                  XRS[:, (tg % 2) * 512:(tg % 2 + 1) * 512],
                                  op=ALU.add).then_inc(pe.h, 1)
                tg += 1

    def pr_g(g, pe=pe, pe_base=pe_base, so=so):
        tg = 0
        for qc in range(4):
            for fg in range(8):
                g.e.wait_ge(pe.h, pe_base + tg + 1)
                g.e.dma_start(out=io["xoT"][fg * 128:(fg + 1) * 128,
                                            qc * 512:(qc + 1) * 512],
                              in_=XOS[:, (tg % 2) * 512:(tg % 2 + 1) * 512])\
                    .then_inc(so[tg % 2].h, 16)
                tg += 1
    k.on("tensor", pr_t)
    k.on("sync", pr_sy)
    k.on("vector", pr_v)
    k.on("gpsimd", pr_g)


def build_l0a_A(pools=True, dov=True):
    import contextlib
    nc = bass.Bass("TRN2", target_bir_lowering=False)
    io = {}
    io["xT"] = nc.dram_tensor("xT", [DM, TOK], F32, kind="ExternalInput").ap()
    for nm in ("wq", "wk", "wv"):
        io[nm] = nc.dram_tensor(nm, [1024, 1024], F32, kind="ExternalInput").ap()
    io["qT"] = nc.dram_tensor("qT", [DM, TOK], BF16, kind="ExternalOutput").ap()
    io["kT"] = nc.dram_tensor("kT", [DM, TOK], BF16, kind="ExternalOutput").ap()
    io["vtm"] = nc.dram_tensor("vtm", [TOK, DM], BF16, kind="ExternalOutput").ap()
    io["qp"] = nc.dram_tensor("qp", [128, 512], F32, kind="ExternalOutput").ap()
    io["kp"] = nc.dram_tensor("kp", [128, 512], F32, kind="ExternalOutput").ap()
    with contextlib.ExitStack() as stack:
        sb = lambda n, s, dt=F32: stack.enter_context(nc.sbuf_tensor(n, s, dt))
        ps = lambda n, s, dt=F32: stack.enter_context(nc.psum_tensor(n, s, dt))
        with nc.Block() as block:
            k = Kernel(nc, stack)
            emit_l0a_A(k, sb, ps, io, pools=pools, dov=dov)
            k.finish(block)
    return nc


def build_l0a_B():
    import contextlib
    nc = bass.Bass("TRN2", target_bir_lowering=False)
    io = {}
    io["qp_all"] = nc.dram_tensor("qp_all", [128, 4096], F32, kind="ExternalInput").ap()
    io["kp_all"] = nc.dram_tensor("kp_all", [128, 4096], F32, kind="ExternalInput").ap()
    io["msk"] = nc.dram_tensor("msk", [128, 512], BF16, kind="ExternalOutput").ap()
    with contextlib.ExitStack() as stack:
        sb = lambda n, s, dt=F32: stack.enter_context(nc.sbuf_tensor(n, s, dt))
        ps = lambda n, s, dt=F32: stack.enter_context(nc.psum_tensor(n, s, dt))
        with nc.Block() as block:
            k = Kernel(nc, stack)
            emit_l0a_B(k, sb, ps, io)
            k.finish(block)
    return nc


def build_l0a_C(nh=16):
    import contextlib
    nc = bass.Bass("TRN2", target_bir_lowering=False)
    io = {}
    io["qT"] = nc.dram_tensor("qT", [DM, TOK], BF16, kind="ExternalInput").ap()
    io["kTb"] = nc.dram_tensor("kTb", [DM, 8192], BF16, kind="ExternalInput").ap()
    io["vtmb"] = nc.dram_tensor("vtmb", [8192, DM], BF16, kind="ExternalInput").ap()
    io["mi240"] = nc.dram_tensor("mi240", [64, 8192], BF16, kind="ExternalInput").ap()
    io["exq"] = nc.dram_tensor("exq", [64, TOK], BF16, kind="ExternalInput").ap()
    io["wo"] = nc.dram_tensor("wo", [1024, 1024], F32, kind="ExternalInput").ap()
    io["xT"] = nc.dram_tensor("xT", [DM, TOK], F32, kind="ExternalInput").ap()
    io["xoT"] = nc.dram_tensor("xoT", [DM, TOK], F32, kind="ExternalOutput").ap()
    with contextlib.ExitStack() as stack:
        sb = lambda n, s, dt=F32: stack.enter_context(nc.sbuf_tensor(n, s, dt))
        ps = lambda n, s, dt=F32: stack.enter_context(nc.psum_tensor(n, s, dt))
        with nc.Block() as block:
            k = Kernel(nc, stack)
            emit_l0a_C(k, sb, ps, io, nh=nh)
            k.finish(block)
    return nc


def emit_moe_M1(k, sb, ps, io):
    T = TOK
    Xc = [sb(f"m1Xc{i}", [128, 512]) for i in range(8)]
    XF = [sb(f"m1XF{i}", [128, T]) for i in range(8)]
    gw = sb("m1gw", [128, 64])
    bgb = sb("m1bgb", [128, 8])
    PRB = sb("m1PRB", [128, 16 * 8])
    SCR = sb("m1SCR", [128, 8])
    M1c = sb("m1M1", [128, 4])
    NOTK = sb("m1NOTK", [128, 1])
    PICK = sb("m1PICK", [128, 8])
    PSUP = sb("m1PSUP", [128, 8])
    OH = [sb(f"m1OH{c}", [128, 16 * 8], BF16) for c in range(2)]
    OHF = [sb(f"m1OHF{c}", [8, T]) for c in range(2)]
    PROBF = sb("m1PROBF", [8, T])
    LPF = sb("m1LPF", [8, T])
    PRODF = sb("m1PRODF", [8, T])
    IDEf = sb("m1IDEf", [128, 128])
    IDEb = sb("m1IDEb", [128, 128], BF16)
    onest = sb("m1onest", [8, T])
    onesc = sb("m1onesc", [8, 1])
    CNTF = sb("m1CNTF", [8, 2])
    ps_a = ps("m1ps_a", [128, 512])
    ps_b = ps("m1ps_b", [128, 512])
    ps_g = ps("m1ps_g", [128, 8])
    ps_fb = ps("m1ps_fb", [8, 128], BF16)
    ps_ff = ps("m1ps_ff", [8, 128])
    auxf = ln_aux_alloc(sb, ps, 512, "m1f", F32, ps_sum=ps_a, ps_sq=ps_b)

    k.phase("m1_init")
    k.on("sync", lambda sy: (sy.dma(bgb[:, :], io["bgb"][:, :]),
                             sy.dma(IDEf[:, :], io["IDE"][:, :])))
    k.on("gpsimd", lambda g: g.dma(gw[:, :], io["gate_wT"][:, :]))
    k.on("vector", lambda v: (v.e.memset(auxf["ones"][:, :], 1.0),
                              v.e.memset(onest[:, :], 1.0),
                              v.e.memset(onesc[:, :], 1.0)))
    k.phase("m1_init2")
    k.on("vector", lambda v: v.e.tensor_copy(IDEb[:, :], IDEf[:, :]))

    for tc in range(4):
        tsl = slice(tc * 512, (tc + 1) * 512)
        k.phase(f"m1_ld{tc}")

        def ld_x(sy, tsl=tsl):
            for ki in range(8):
                sy.dma(Xc[ki][:, :], io["xT"][ki * 128:(ki + 1) * 128, tsl])
        k.on("sync", ld_x)
        k.phase(f"m1_ln{tc}")
        emit_ln(k, f"m1_ln{tc}", Xc, [x[:, tsl] for x in XF], 512, auxf)

    k.phase("m1_stxh")

    def st_xh(g):
        for ki in range(8):
            g.dma(io["xhb"][ki * 128:(ki + 1) * 128, :], XF[ki][:, :])
    k.on("gpsimd", st_xh)

    for ti in range(16):
        k.phase(f"m1_gate{ti}")
        gg = k.sem("m1_gg")
        gg_done = gg.cnt + 1
        gg.expect(1)

        def gate_t(t, ti=ti, gg=gg):
            ins = None
            for ki in range(8):
                ins = t.e.matmul(ps_g[:, :], XF[ki][:, ti * 128:(ti + 1) * 128],
                                 gw[:, ki * 8:(ki + 1) * 8],
                                 start=(ki == 0), stop=(ki == 7))
            ins.then_inc(gg.h, 1)
        k.on("tensor", gate_t)

        def gate_v(v, ti=ti, gg=gg, gg_done=gg_done):
            v.e.wait_ge(gg.h, gg_done)
            P = PRB[:, ti * 8:(ti + 1) * 8]
            v.mark(v.e.tensor_tensor(P, ps_g[:, :], bgb[:, :], op=ALU.add))
            v.join()
            v.mark(v.e.reduce_max(M1c[:, 0:1], P, axis=AX.X))
            v.join()
            v.mark(v.e.tensor_scalar_mul(M1c[:, 1:2], M1c[:, 0:1], -1.0))
            v.join()
        k.on("vector", gate_v)
        k.phase(f"m1_gexp{ti}")
        k.on("scalar", (lambda ti=ti: (lambda s: s.e.activation(
            SCR[:, :], PRB[:, ti * 8:(ti + 1) * 8], AF.Exp,
            bias=M1c[:, 1:2])))())
        k.phase(f"m1_gnorm{ti}")

        def gnorm_v(v, ti=ti):
            P = PRB[:, ti * 8:(ti + 1) * 8]
            v.mark(v.e.reduce_sum(M1c[:, 2:3], SCR[:, :], axis=AX.X))
            v.join()
            v.mark(v.e.reciprocal(M1c[:, 3:4], M1c[:, 2:3]))
            v.join()
            v.mark(v.e.tensor_scalar_mul(P, SCR[:, :], M1c[:, 3:4]))
            v.join()
            for c in range(2):
                src = P if c == 0 else PSUP[:, :]
                v.mark(v.e.reduce_max(M1c[:, 0:1], src, axis=AX.X))
                v.mark(v.e.memset(NOTK[:, :], 1.0))
                v.join()
                for e in range(8):
                    v.mark(v.e.tensor_tensor(SCR[:, e:e + 1], src[:, e:e + 1],
                                             M1c[:, 0:1], op=ALU.is_ge))
                    v.join()
                    v.mark(v.e.tensor_tensor(PICK[:, e:e + 1], SCR[:, e:e + 1],
                                             NOTK[:, :], op=ALU.mult))
                    v.join()
                    v.mark(v.e.tensor_tensor(NOTK[:, :], NOTK[:, :],
                                             PICK[:, e:e + 1], op=ALU.subtract))
                    v.join()
                v.mark(v.e.tensor_copy(OH[c][:, ti * 8:(ti + 1) * 8], PICK[:, :]))
                v.join()
                if c == 0:
                    v.mark(v.e.scalar_tensor_tensor(PSUP[:, :], PICK[:, :], -1e9,
                                                    P, op0=ALU.mult, op1=ALU.add))
                    v.join()
        k.on("vector", gnorm_v)

    k.phase("m1_tr")
    trq = k.sem("m1_trq")
    tre = k.sem("m1_tre")
    trq_base, tre_base = trq.cnt, tre.cnt
    trq.expect(48)
    tre.expect(48)

    def tr_t(t, trq=trq, tre=tre, tre_base=tre_base):
        i = 0
        for c in range(2):
            for ti in range(16):
                if i >= 1:
                    t.e.wait_ge(tre.h, tre_base + i)
                t.e.transpose(ps_fb[:, :], OH[c][:, ti * 8:(ti + 1) * 8],
                              IDEb[:, :]).then_inc(trq.h, 1)
                i += 1
        for ti in range(16):
            if i >= 1:
                t.e.wait_ge(tre.h, tre_base + i)
            t.e.transpose(ps_ff[:, :], PRB[:, ti * 8:(ti + 1) * 8], IDEf[:, :])\
                .then_inc(trq.h, 1)
            i += 1

    def tr_v(v, trq=trq, tre=tre, trq_base=trq_base):
        i = 0
        for c in range(2):
            for ti in range(16):
                v.e.wait_ge(trq.h, trq_base + i + 1)
                v.e.tensor_copy(OHF[c][:, ti * 128:(ti + 1) * 128], ps_fb[:, :])\
                    .then_inc(tre.h, 1)
                i += 1
        for ti in range(16):
            v.e.wait_ge(trq.h, trq_base + i + 1)
            v.e.tensor_copy(PROBF[:, ti * 128:(ti + 1) * 128], ps_ff[:, :])\
                .then_inc(tre.h, 1)
            i += 1
    k.on("tensor", tr_t)
    k.on("vector", tr_v)

    def mk_scan(c):
        def scan_v(v):
            v.mark(v.e.tensor_tensor_scan(LPF[:, :], onest[:, :], OHF[c][:, :],
                                          0.0, op0=ALU.mult, op1=ALU.add))
            v.join()
            v.mark(v.e.tensor_copy(CNTF[:, c:c + 1], LPF[:, T - 1:T]))
            v.join()
            v.mark(v.e.tensor_sub(LPF[:, :], LPF[:, :], OHF[c][:, :]))
            v.join()
            v.mark(v.e.tensor_tensor(PRODF[:, :], OHF[c][:, :], PROBF[:, :],
                                     op=ALU.mult))
            v.join()
        return scan_v

    def mk_strows(c):
        def strows(g):
            g.dma(io["lposf"][:, c * T:(c + 1) * T], LPF[:, :])
            g.dma(io["ohprb"][:, c * T:(c + 1) * T], PRODF[:, :])
        return strows

    k.phase("m1_scan0")
    k.on("vector", mk_scan(0))
    k.phase("m1_rows0")
    k.on("gpsimd", mk_strows(0))
    k.phase("m1_scan1")
    k.on("vector", mk_scan(1))
    k.phase("m1_rows1")
    k.on("gpsimd", mk_strows(1))

    k.phase("m1_st")

    def st(g):
        g.dma(io["cnt"][:, :], CNTF[:, :])
    k.on("gpsimd", st)


def emit_moe_M2(k, sb, ps, io):
    T = TOK
    XHB = [sb(f"m2XH{i}", [128, T], BF16) for i in range(8)]
    YACC = [sb(f"m2YA{i}", [128, T]) for i in range(8)]
    HBOG = sb("m2HBOG", [128, 4 * T], BF16)
    WB1 = [sb(f"m2WB1_{i}", [128, 8 * 512], BF16) for i in range(2)]
    WB2 = [sb(f"m2WB2_{i}", [128, 4 * 1024], BF16) for i in range(2)]
    WF = sb("m2WF", [128, T])
    WFALL = sb("m2WFALL", [8, T])
    KEEP = WF[0:8, 0:T]
    LPOS = [WB1[i].bitcast(F32)[0:8, 0:T] for i in range(2)]
    OHPR = [WB2[i].bitcast(F32)[0:8, 0:T] for i in range(2)]
    SEL8 = sb("m2SEL8", [8, 1024])
    TMPS = sb("m2TMPS", [128, 2 * 512])
    B1R = sb("m2B1R", [128, 8 * 32])
    B2R = sb("m2B2R", [128, 8 * 8])
    PREF = sb("m2PREF", [8, 14])
    GS = sb("m2GS", [8, 2])
    ps_m1 = [ps(f"m2ps1_{i}", [128, 512]) for i in range(4)]
    ps_m2 = [ps(f"m2ps2_{i}", [128, 512]) for i in range(4)]

    k.phase("m2_init")

    def init_ld(sy):
        for c in range(2):
            sy.dma(LPOS[c][:, :], io["lposf"][:, c * T:(c + 1) * T])
            sy.dma(OHPR[c][:, :], io["ohprb"][:, c * T:(c + 1) * T])
        sy.dma(PREF[:, :], io["pref"][:, :])
        sy.dma(SEL8[:, :], io["sel8"][:, :])
        sy.dma(B1R[:, :], io["b1r"][:, :])
        sy.dma(B2R[:, :], io["b2r"][:, :])
        for ki in range(8):
            sy.dma(XHB[ki][:, :], io["xhb"][ki * 128:(ki + 1) * 128, :])
    k.on("sync", init_ld)
    k.on("vector", lambda v: [v.e.memset(YACC[i][:, :], 0.0) for i in range(8)])

    k.phase("m2_wfall")

    def wfall_v(v):
        v.mark(v.e.memset(GS[:, :], 0.0))
        v.join()
        for j in range(7):
            v.mark(v.e.tensor_add(GS[:, :], GS[:, :], PREF[:, 2 * j:2 * j + 2]))
            v.join()
        v.mark(v.e.tensor_scalar(KEEP[:, :], LPOS[0][:, :], GS[:, 0:1], 2560.0,
                                 op0=ALU.add, op1=ALU.is_lt))
        v.join()
        v.mark(v.e.tensor_tensor(WFALL[:, :], KEEP[:, :], OHPR[0][:, :],
                                 op=ALU.mult))
        v.join()
        v.mark(v.e.tensor_scalar(KEEP[:, :], LPOS[1][:, :], GS[:, 1:2],
                                 2560.0, op0=ALU.add, op1=ALU.is_lt))
        v.join()
        v.mark(v.e.tensor_tensor(KEEP[:, :], KEEP[:, :], OHPR[1][:, :],
                                 op=ALU.mult))
        v.join()
        v.mark(v.e.tensor_add(WFALL[:, :], WFALL[:, :], KEEP[:, :]))
        v.join()
    k.on("vector", wfall_v)

    for e in range(8):
        k.phase(f"m2_wfb{e}")
        wfb = k.sem("m2_wfb")
        wfb_base = wfb.cnt
        wfb.expect(4)

        def wfb_t(t, e=e, wfb=wfb):
            for tn in range(4):
                t.e.matmul(ps_m1[tn][:, :], SEL8[:, e * 128:(e + 1) * 128],
                           WFALL[:, tn * 512:(tn + 1) * 512],
                           start=True, stop=True).then_inc(wfb.h, 1)

        def wfb_v(v, wfb=wfb, wfb_base=wfb_base):
            for tn in range(4):
                v.e.wait_ge(wfb.h, wfb_base + tn + 1)
                v.e.tensor_copy(WF[:, tn * 512:(tn + 1) * 512], ps_m1[tn][:, :])
        k.on("tensor", wfb_t)
        k.on("vector", wfb_v)

        k.phase(f"m2_e{e}")
        w1l = [k.sem("m2_w1l0"), k.sem("m2_w1l1")]
        w2l = [k.sem("m2_w2l0"), k.sem("m2_w2l1")]
        m1p = k.sem("m2_m1p")
        ev1 = k.sem("m2_ev1")
        m2p = k.sem("m2_m2p")
        ev2 = k.sem("m2_ev2")
        ra = k.sem("m2_ra")
        vm = k.sem("m2_vm")
        w1l_base = [w1l[0].cnt, w1l[1].cnt]
        w2l_base = [w2l[0].cnt, w2l[1].cnt]
        m1p_base, ev1_base = m1p.cnt, ev1.cnt
        m2p_base, ev2_base, ra_base = m2p.cnt, ev2.cnt, ra.cnt
        vm_base = vm.cnt
        for og in range(8):
            w1l[og % 2].expect(8 * 16)
            w2l[og % 2].expect(4 * 16)
        m1p.expect(128)
        ev1.expect(128)
        m2p.expect(256)
        ev2.expect(256)
        ra.expect(256)
        vm.expect(256)

        def eload(g, e=e, w1l=w1l, w2l=w2l, m1p=m1p, m2p=m2p,
                  m1p_base=m1p_base, m2p_base=m2p_base):
            for og in range(8):
                b = og % 2
                if og >= 2:
                    g.e.wait_ge(m1p.h, m1p_base + (og - 1) * 16)
                for ki in range(8):
                    g.e.dma_start(
                        out=WB1[b][:, ki * 512:(ki + 1) * 512],
                        in_=io["w1c"][e * 1024 + ki * 128:e * 1024 + (ki + 1) * 128,
                                      og * 512:(og + 1) * 512],
                    ).then_inc(w1l[b].h, 16)
                if og >= 2:
                    g.e.wait_ge(m2p.h, m2p_base + (og - 1) * 32)
                for kk in range(4):
                    g.e.dma_start(
                        out=WB2[b][:, kk * 1024:(kk + 1) * 1024],
                        in_=io["w2c"][e * 4096 + og * 512 + kk * 128:
                                      e * 4096 + og * 512 + (kk + 1) * 128, :],
                    ).then_inc(w2l[b].h, 16)

        def emm(t, w1l=w1l, w2l=w2l, m1p=m1p, ev1=ev1, m2p=m2p, ev2=ev2,
                w1l_base=w1l_base, w2l_base=w2l_base, ev1_base=ev1_base,
                ev2_base=ev2_base):
            for og in range(8):
                b = og % 2
                t.e.wait_ge(w1l[b].h, w1l_base[b] + 128 * (og // 2 + 1))
                for mi in range(4):
                    for tn in range(4):
                        idx = og * 16 + mi * 4 + tn
                        if idx >= 4:
                            t.e.wait_ge(ev1.h, ev1_base + idx - 3)
                        ins = None
                        for ki in range(8):
                            ins = t.e.matmul(
                                ps_m1[idx % 4][:, :],
                                WB1[b][:, ki * 512 + mi * 128:ki * 512 + (mi + 1) * 128],
                                XHB[ki][:, tn * 512:(tn + 1) * 512],
                                start=(ki == 0), stop=(ki == 7))
                        ins.then_inc(m1p.h, 1)
                t.e.wait_ge(w2l[b].h, w2l_base[b] + 64 * (og // 2 + 1))
                t.e.wait_ge(ev1.h, ev1_base + (og + 1) * 16)
                for tn in range(4):
                    for mo in range(8):
                        idx2 = og * 32 + tn * 8 + mo
                        if idx2 >= 4:
                            t.e.wait_ge(ev2.h, ev2_base + idx2 - 3)
                        ins = None
                        for mi in range(4):
                            ins = t.e.matmul(
                                ps_m2[idx2 % 4][:, :],
                                WB2[b][:, mi * 1024 + mo * 128:mi * 1024 + (mo + 1) * 128],
                                HBOG[:, mi * T + tn * 512:mi * T + (tn + 1) * 512],
                                start=(mi == 0), stop=(mi == 3))
                        ins.then_inc(m2p.h, 1)

        def eevac(s, e=e, m1p=m1p, ev1=ev1, m2p=m2p, ev2=ev2, ra=ra,
                  m1p_base=m1p_base, m2p_base=m2p_base, ra_base=ra_base):
            for og in range(8):
                if og >= 1:
                    s.e.wait_ge(m2p.h, m2p_base + og * 32)
                for mi in range(4):
                    for tn in range(4):
                        idx = og * 16 + mi * 4 + tn
                        s.e.wait_ge(m1p.h, m1p_base + idx + 1)
                        s.e.activation(
                            HBOG[:, mi * T + tn * 512:mi * T + (tn + 1) * 512],
                            ps_m1[idx % 4][:, :], AF.Relu,
                            bias=B1R[:, e * 32 + og * 4 + mi:
                                     e * 32 + og * 4 + mi + 1],
                        ).then_inc(ev1.h, 1)
                for tn in range(4):
                    for mo in range(8):
                        idx2 = og * 32 + tn * 8 + mo
                        s.e.wait_ge(m2p.h, m2p_base + idx2 + 1)
                        if idx2 >= 2:
                            s.e.wait_ge(ra.h, ra_base + idx2 - 1)
                        dst = TMPS[:, (idx2 % 2) * 512:(idx2 % 2 + 1) * 512]
                        if og == 0:
                            ins = s.e.activation(
                                dst, ps_m2[idx2 % 4][:, :], AF.Identity,
                                bias=B2R[:, e * 8 + mo:e * 8 + mo + 1])
                        else:
                            ins = s.e.activation(dst, ps_m2[idx2 % 4][:, :],
                                                 AF.Identity)
                        ins.then_inc(ev2.h, 1)

        def eacc(v, ev2=ev2, ra=ra, vm=vm, ev2_base=ev2_base,
                 ra_base=ra_base, vm_base=vm_base):
            for og in range(8):
                for tn in range(4):
                    for mo in range(8):
                        idx2 = og * 32 + tn * 8 + mo
                        v.e.wait_ge(ev2.h, ev2_base + idx2 + 1)
                        if idx2 >= 1:
                            v.e.wait_ge(ra.h, ra_base + idx2)
                        tt = TMPS[:, (idx2 % 2) * 512:(idx2 % 2 + 1) * 512]
                        v.e.tensor_tensor(
                            tt, tt, WF[:, tn * 512:(tn + 1) * 512],
                            op=ALU.mult).then_inc(vm.h, 1)
                        v.e.wait_ge(vm.h, vm_base + idx2 + 1)
                        v.e.tensor_add(
                            YACC[mo][:, tn * 512:(tn + 1) * 512],
                            YACC[mo][:, tn * 512:(tn + 1) * 512], tt)\
                            .then_inc(ra.h, 1)
        k.on("gpsimd", eload)
        k.on("tensor", emm)
        k.on("scalar", eevac)
        k.on("vector", eacc)

    Xst = HBOG.bitcast(F32)
    k.phase("m2_resX")
    xl = [k.sem("m2_xl0"), k.sem("m2_xl1")]
    xa = k.sem("m2_xa")
    xl_base = [xl[0].cnt, xl[1].cnt]
    xa_base = xa.cnt
    for ki in range(8):
        xl[ki % 2].expect(16)
    xa.expect(8)

    def ldx(sy, xl=xl, xa=xa, xa_base=xa_base):
        for ki in range(8):
            if ki >= 2:
                sy.e.wait_ge(xa.h, xa_base + ki - 1)
            sy.e.dma_start(out=Xst[:, (ki % 2) * T:(ki % 2 + 1) * T],
                           in_=io["xT"][ki * 128:(ki + 1) * 128, :])\
                .then_inc(xl[ki % 2].h, 16)

    def resf(v, xl=xl, xa=xa, xl_base=xl_base):
        for ki in range(8):
            v.e.wait_ge(xl[ki % 2].h, xl_base[ki % 2] + 16 * (ki // 2 + 1))
            v.e.tensor_add(YACC[ki][:, :], YACC[ki][:, :],
                           Xst[:, (ki % 2) * T:(ki % 2 + 1) * T])\
                .then_inc(xa.h, 1)
    k.on("sync", ldx)
    k.on("vector", resf)
    k.phase("m2_st")

    def st(g):
        for ki in range(8):
            g.dma(io["xoT"][ki * 128:(ki + 1) * 128, :], YACC[ki][:, :])
    k.on("gpsimd", st)


def build_moe_M1():
    import contextlib
    nc = bass.Bass("TRN2", target_bir_lowering=False)
    io = {}
    io["xT"] = nc.dram_tensor("xT", [DM, TOK], F32, kind="ExternalInput").ap()
    for nm, shp in (("gate_wT", [128, 64]), ("bgb", [128, 8]), ("IDE", [128, 128])):
        io[nm] = nc.dram_tensor(nm, shp, F32, kind="ExternalInput").ap()
    io["xhb"] = nc.dram_tensor("xhb", [DM, TOK], BF16, kind="ExternalOutput").ap()
    io["lposf"] = nc.dram_tensor("lposf", [8, 2 * TOK], F32, kind="ExternalOutput").ap()
    io["ohprb"] = nc.dram_tensor("ohprb", [8, 2 * TOK], F32, kind="ExternalOutput").ap()
    io["cnt"] = nc.dram_tensor("cnt", [8, 2], F32, kind="ExternalOutput").ap()
    with contextlib.ExitStack() as stack:
        sb = lambda n, s, dt=F32: stack.enter_context(nc.sbuf_tensor(n, s, dt))
        ps = lambda n, s, dt=F32: stack.enter_context(nc.psum_tensor(n, s, dt))
        with nc.Block() as block:
            k = Kernel(nc, stack)
            emit_moe_M1(k, sb, ps, io)
            k.finish(block)
    return nc


def build_moe_M2():
    import contextlib
    nc = bass.Bass("TRN2", target_bir_lowering=False)
    io = {}
    io["xT"] = nc.dram_tensor("xT", [DM, TOK], F32, kind="ExternalInput").ap()
    for nm, shp, dt in (("xhb", [DM, TOK], BF16), ("lposf", [8, 2 * TOK], F32),
                        ("ohprb", [8, 2 * TOK], F32), ("pref", [8, 14], F32),
                        ("sel8", [8, 1024], F32), ("b1r", [128, 256], F32),
                        ("b2r", [128, 64], F32), ("w1c", [8192, 4096], BF16),
                        ("w2c", [8 * 4096, 1024], BF16)):
        io[nm] = nc.dram_tensor(nm, shp, dt, kind="ExternalInput").ap()
    io["xoT"] = nc.dram_tensor("xoT", [DM, TOK], F32, kind="ExternalOutput").ap()
    with contextlib.ExitStack() as stack:
        sb = lambda n, s, dt=F32: stack.enter_context(nc.sbuf_tensor(n, s, dt))
        ps = lambda n, s, dt=F32: stack.enter_context(nc.psum_tensor(n, s, dt))
        with nc.Block() as block:
            k = Kernel(nc, stack)
            emit_moe_M2(k, sb, ps, io)
            k.finish(block)
    return nc


_NC_CACHE = {}


def _get_nc(name, builder):
    if name not in _NC_CACHE:
        _NC_CACHE[name] = builder()
    return _NC_CACHE[name]


def _run(nc, in_maps):
    r = run_bass_kernel_spmd(nc, in_maps, core_ids=list(range(NCORES)))
    return r.results


def _step_ffn(kind, xT, qw, inp):
    nc = _get_nc("ffn_" + kind, lambda: build_ffn_step(kind))
    if kind == "L0":
        extra = {"ln1g": col_major(inp["L0_ffn_ln1_g"]),
                 "ln1b": col_major(inp["L0_ffn_ln1_b"]),
                 "ln2g": col_major(inp["L0_ffn_ln2_g"]),
                 "ln2b": col_major(inp["L0_ffn_ln2_b"])}
        b1r, b2r = col_major(inp["L0_ffn_b1"]), col_major(inp["L0_ffn_b2"])
        w1q, w2q = qw["L0w1_q"], qw["L0w2_q"]
    else:
        extra = {}
        b1r, b2r = col_major(inp["L2_ffn_b1"]), col_major(inp["L2_ffn_b2"])
        w1q, w2q = qw["L2w1_q"], qw["L2w2_q"]
    in_maps = []
    for c in range(NCORES):
        m = {"xT": xT[c], "w1q": w1q, "w2q": w2q, "b1r": b1r, "b2r": b2r}
        m.update(extra)
        in_maps.append(m)
    res = _run(nc, in_maps)
    return [np.ascontiguousarray(res[c]["xoT"]) for c in range(NCORES)]


def _step_l0a(xT, inp):
    import ml_dtypes
    ncA = _get_nc("l0aA", build_l0a_A)
    wA = {"wq": np.ascontiguousarray(np.asarray(inp["L0_attn_W_Q"], np.float32)),
          "wk": np.ascontiguousarray(np.asarray(inp["L0_attn_W_K"], np.float32)),
          "wv": np.ascontiguousarray(np.asarray(inp["L0_attn_W_V"], np.float32))}
    in_maps = []
    for c in range(NCORES):
        m = {"xT": xT[c]}
        m.update(wA)
        in_maps.append(m)
    resA = _run(ncA, in_maps)

    def pools(key):
        out = np.zeros((128, 4096), np.float32)
        for b in range(2):
            for hp in range(8):
                for cc in range(4):
                    c = b * 4 + cc
                    out[:, (b * 8 + hp) * 256 + cc * 64:
                        (b * 8 + hp) * 256 + (cc + 1) * 64] = \
                        resA[c][key][:, hp * 64:(hp + 1) * 64]
        return out
    ncB = _get_nc("l0aB", build_l0a_B)
    qp_all, kp_all = pools("qp"), pools("kp")
    resB = _run(ncB, [{"qp_all": qp_all, "kp_all": kp_all}
                      for _ in range(NCORES)])
    msk = resB[0]["msk"]
    mask = np.concatenate([msk[:, 0:256], msk[:, 256:512]], axis=0)

    ncC = _get_nc("l0aC", build_l0a_C)
    exq = np.zeros((64, 2048), np.float32)
    for i in range(64):
        exq[i, i * 32:(i + 1) * 32] = 1.0
    exq = exq.astype(ml_dtypes.bfloat16)
    wo = np.ascontiguousarray(np.asarray(inp["L0_attn_W_O"], np.float32))
    in_mapsC = []
    for c in range(NCORES):
        b = c // 4
        kTb = np.concatenate([resA[b * 4 + j]["kT"] for j in range(4)], axis=1)
        vtmb = np.concatenate([resA[b * 4 + j]["vtm"] for j in range(4)], axis=0)
        mrows = mask[(c % 4) * 64:(c % 4 + 1) * 64, :]
        mi240 = np.repeat(mrows, 32, axis=1)
        in_mapsC.append({"qT": resA[c]["qT"], "kTb": np.ascontiguousarray(kTb),
                         "vtmb": np.ascontiguousarray(vtmb),
                         "mi240": np.ascontiguousarray(mi240), "exq": exq,
                         "wo": wo, "xT": xT[c]})
    res = _run(ncC, in_mapsC)
    return [np.ascontiguousarray(res[c]["xoT"]) for c in range(NCORES)]


def _step_moe(xT, qw, inp):
    ncA = _get_nc("moeM1", build_moe_M1)
    IDE = np.eye(128, dtype=np.float32)
    gate_b = np.asarray(inp["L1_ffn_gate_b"], np.float32)
    ioA = {"gate_wT": fm128(np.asarray(inp["L1_ffn_gate_w"], np.float32).T),
           "bgb": np.ascontiguousarray(np.tile(gate_b[None, :], (128, 1))),
           "IDE": IDE}
    in_maps = []
    for c in range(NCORES):
        m = {"xT": xT[c]}
        m.update(ioA)
        in_maps.append(m)
    resA = _run(ncA, in_maps)

    ncB = _get_nc("moeM2", build_moe_M2)
    b1 = np.asarray(inp["L1_ffn_b1"], np.float32)
    b2 = np.asarray(inp["L1_ffn_b2"], np.float32)
    b1r = np.ascontiguousarray(
        np.concatenate([col_major(b1[e]) for e in range(8)], axis=1))
    b2r = np.ascontiguousarray(
        np.concatenate([col_major(b2[e]) for e in range(8)], axis=1))
    sel8 = np.zeros((8, 1024), np.float32)
    for e in range(8):
        sel8[e, e * 128:(e + 1) * 128] = 0.5
    ioB = {"w1c": qw["M1w1_c"], "w2c": qw["M1w2_c"], "sel8": sel8,
           "b1r": b1r, "b2r": b2r}
    in_mapsB = []
    for c in range(NCORES):
        pref = np.zeros((8, 14), np.float32)
        for j in range(c):
            pref[:, 2 * j:2 * j + 2] = resA[j]["cnt"]
        m = {"xT": xT[c], "xhb": resA[c]["xhb"], "lposf": resA[c]["lposf"],
             "ohprb": resA[c]["ohprb"], "pref": pref}
        m.update(ioB)
        in_mapsB.append(m)
    res = _run(ncB, in_mapsB)
    return [np.ascontiguousarray(res[c]["xoT"]) for c in range(NCORES)]


def _step_l1a(xT, inp):
    qkv_b = np.asarray(inp["L1_attn_qkv_b"], np.float32)
    bqkv = np.concatenate([col_major(qkv_b[0:1024]),
                           col_major(qkv_b[1024:2048])], axis=1)
    UTRI = (np.arange(128)[:, None] <= np.arange(128)[None, :]).astype(np.float32)
    SELH = np.zeros((16, 1024), np.float32)
    for mo in range(8):
        for p in range(128):
            SELH[2 * mo + p // 64, mo * 128 + p] = 1.0
    ioA = {"qkv_wT": np.ascontiguousarray(
               np.asarray(inp["L1_attn_qkv_w"], np.float32).T),
           "bqkv": np.ascontiguousarray(bqkv), "UTRI": UTRI,
           "bvb": np.ascontiguousarray(np.tile(qkv_b[2048:3072][None, :], (128, 1))),
           "bkb": np.ascontiguousarray(np.tile(qkv_b[1024:2048][None, :], (128, 1)))}
    ncA = _get_nc("l1aA", build_l1a_A)
    in_maps = []
    for c in range(NCORES):
        m = {"xT": xT[c]}
        m.update(ioA)
        in_maps.append(m)
    resA = _run(ncA, in_maps)

    ncB = _get_nc("l1aB", build_l1a_B)
    ioB = {"SELH": SELH,
           "out_wT": fm128(np.asarray(inp["L1_attn_out_w"], np.float32).T),
           "bo": col_major(inp["L1_attn_out_b"])}
    zkv = np.zeros((64, 1024), np.float32)
    zkc = np.zeros((64, 16), np.float32)
    in_mapsB = []
    for c in range(NCORES):
        gidx = c % 4
        gb = (c // 4) * 4
        gKV = np.concatenate([resA[gb + j]["KVT"] if j < gidx else zkv
                              for j in range(4)], axis=1)
        gKC = np.concatenate([resA[gb + j]["KCT"] if j < gidx else zkc
                              for j in range(4)], axis=1)
        m = {"xT": xT[c], "NUM": resA[c]["NUM"], "DEN": resA[c]["DEN"],
             "QTb": resA[c]["QTb"], "gKV": np.ascontiguousarray(gKV),
             "gKC": np.ascontiguousarray(gKC)}
        m.update(ioB)
        in_mapsB.append(m)
    res = _run(ncB, in_mapsB)
    return [np.ascontiguousarray(res[c]["xoT"]) for c in range(NCORES)]


def _step_l2a(xT, inp):
    b_in = np.asarray(inp["L2_attn_b_in"], np.float32)
    ioA = {
        "w_inT": np.ascontiguousarray(np.asarray(inp["L2_attn_w_in"], np.float32).T),
        "w_dtT": np.ascontiguousarray(np.asarray(inp["L2_attn_w_dt"], np.float32).T),
        "w_BT": np.ascontiguousarray(np.asarray(inp["L2_attn_w_B"], np.float32).T),
        "w_CT": np.ascontiguousarray(np.asarray(inp["L2_attn_w_C"], np.float32).T),
        "bx": col_major(b_in[:1024]), "bz": col_major(b_in[1024:]),
        "bdt": col_major(inp["L2_attn_b_dt"]),
        "bB": np.ascontiguousarray(np.asarray(inp["L2_attn_b_B"], np.float32).reshape(16, 1)),
        "bC": np.ascontiguousarray(np.asarray(inp["L2_attn_b_C"], np.float32).reshape(16, 1)),
        "A_fm": fm128(np.asarray(inp["L2_attn_A_log"], np.float32)),
        "D_fm": col_major(inp["L2_attn_D"]),
        "SEL": sel_const(),
    }
    ncA = _get_nc("l2aA", build_l2a_A)
    in_maps = []
    for c in range(NCORES):
        m = {"xT": xT[c]}
        m.update(ioA)
        in_maps.append(m)
    resA = _run(ncA, in_maps)

    ident = np.ones((128, 128), np.float32)
    zer = np.zeros((128, 128), np.float32)
    ncB = _get_nc("l2aB", build_l2a_B)
    ioB = {
        "A_fm": ioA["A_fm"], "SEL": ioA["SEL"],
        "w_outT": np.ascontiguousarray(np.asarray(inp["L2_attn_w_out"], np.float32).T),
        "bout": col_major(inp["L2_attn_b_out"]),
        "gam": col_major(inp["L2_attn_gamma"]),
        "bet": col_major(inp["L2_attn_beta"]),
    }
    in_mapsB = []
    for c in range(NCORES):
        gidx = c % 4
        gP = np.concatenate(
            [resA[(c // 4) * 4 + j]["Ptot"] if j < gidx else ident
             for j in range(4)], axis=1)
        gB = np.concatenate(
            [resA[(c // 4) * 4 + j]["hend"] if j < gidx else zer
             for j in range(4)], axis=1)
        m = {"xT": xT[c],
             "ys": resA[c]["ys"], "z": resA[c]["z"], "csdt": resA[c]["csdt"],
             "xh": resA[c]["xh"], "CmT": resA[c]["CmT"],
             "gP": np.ascontiguousarray(gP), "gB": np.ascontiguousarray(gB)}
        m.update(ioB)
        in_mapsB.append(m)
    res = _run(ncB, in_mapsB)
    return [np.ascontiguousarray(res[c]["xoT"]) for c in range(NCORES)]


def _step_l3f(xT, inp):
    nc = _get_nc("l3f", build_l3f_step)
    hostio = l3f_host_inputs(inp)
    in_maps = []
    for c in range(NCORES):
        m = {"xT": xT[c]}
        m.update(hostio)
        in_maps.append(m)
    res = _run(nc, in_maps)
    return [np.ascontiguousarray(res[c]["xoT"]) for c in range(NCORES)]


def _step_l3a(xT, inp):
    import ml_dtypes
    D = 0.95
    QSC = np.tile((D ** np.arange(512, dtype=np.float64)).astype(np.float32)[None, :],
                  (128, 1))
    KSCB = np.tile((D ** (2048.0 - np.arange(2560, dtype=np.float64)))
                   .astype(np.float32)[None, :], (128, 1))
    DMSK = np.zeros((128, 2048), np.float32)
    for kd in range(4):
        ii = np.arange(128)[:, None] + kd * 128
        jj = np.arange(512)[None, :]
        DMSK[:, kd * 512:(kd + 1) * 512] = (ii <= jj).astype(np.float32)
    ioA = {
        "WqT": np.ascontiguousarray(np.asarray(inp["L3_attn_Wq"], np.float32)),
        "WkT": np.ascontiguousarray(np.asarray(inp["L3_attn_Wk"], np.float32)),
        "WvT": np.ascontiguousarray(np.asarray(inp["L3_attn_Wv"], np.float32)),
        "bq": col_major(inp["L3_attn_bq"]), "bk": col_major(inp["L3_attn_bk"]),
        "bvb": np.ascontiguousarray(
            np.tile(np.asarray(inp["L3_attn_bv"], np.float32)[None, :], (128, 1))),
        "QSC": QSC,
    }
    ncA = _get_nc("l3aA", build_l3a_A)
    in_maps = []
    for c in range(NCORES):
        m = {"xT": xT[c]}
        m.update(ioA)
        in_maps.append(m)
    resA = _run(ncA, in_maps)

    ncB = _get_nc("l3aB", build_l3a_B)
    zK = np.zeros((1024, 2048), ml_dtypes.bfloat16)
    zV = np.zeros((2048, 1024), ml_dtypes.bfloat16)
    ioB = {"KSCB": KSCB, "DMSK": DMSK,
           "WoT": fm128(np.asarray(inp["L3_attn_Wo"], np.float32)),
           "bo": col_major(inp["L3_attn_bo"])}
    in_mapsB = []
    for c in range(NCORES):
        gidx = c % 4
        prevK = resA[c - 1]["KTo"] if gidx > 0 else zK
        prevV = resA[c - 1]["Vtmo"] if gidx > 0 else zV
        m = {"QdT": resA[c]["QdT"],
             "K2T": np.ascontiguousarray(
                 np.concatenate([prevK, resA[c]["KTo"]], axis=1)),
             "V2": np.ascontiguousarray(
                 np.concatenate([prevV, resA[c]["Vtmo"]], axis=0))}
        m.update(ioB)
        in_mapsB.append(m)
    resB = _run(ncB, in_mapsB)

    ncC = _get_nc("l3aC", build_l3a_C)
    ioC = {"gam": col_major(inp["L3_attn_gamma"]),
           "bet": col_major(inp["L3_attn_beta"])}
    in_mapsC = []
    for c in range(NCORES):
        gb = (c // 4) * 4
        gstats = np.concatenate([resB[gb + j]["stats"] for j in range(4)], axis=1)
        m = {"xT": xT[c], "y": resB[c]["y"],
             "gstats": np.ascontiguousarray(gstats)}
        m.update(ioC)
        in_mapsC.append(m)
    res = _run(ncC, in_mapsC)
    return [np.ascontiguousarray(res[c]["xoT"]) for c in range(NCORES)]


def kernel(**inputs):
    inp = {k: np.asarray(v) for k, v in inputs.items()}
    x = np.asarray(inp["L0_attn_x"], np.float32).reshape(16384, 1024)
    xT = [np.ascontiguousarray(x[c * 2048:(c + 1) * 2048].T)
          for c in range(NCORES)]
    qw = run_prep(inp, _run)
    xT = _step_ffn("L0", xT, qw, inp)
    xT = _step_l0a(xT, inp)
    xT = _step_ffn("L0", xT, qw, inp)
    xT = _step_moe(xT, qw, inp)
    xT = _step_l1a(xT, inp)
    xT = _step_moe(xT, qw, inp)
    xT = _step_ffn("L2", xT, qw, inp)
    xT = _step_l2a(xT, inp)
    xT = _step_ffn("L2", xT, qw, inp)
    xT = _step_l3f(xT, inp)
    xT = _step_l3a(xT, inp)
    xT = _step_l3f(xT, inp)
    out = np.concatenate([t.T for t in xT], axis=0)
    return np.ascontiguousarray(out.reshape(2, 8192, 1024).astype(np.float32))
